# version 22
# baseline (speedup 1.0000x reference)
import os
import numpy as np
import ml_dtypes
from contextlib import ExitStack

import concourse.bass as bass
import concourse.mybir as mybir
from concourse.bass_utils import run_bass_kernel_spmd

F32 = mybir.dt.float32
BF16 = mybir.dt.bfloat16
I32 = mybir.dt.int32
AF = mybir.ActivationFunctionType
OP = mybir.AluOpType
AXX = mybir.AxisListType.X
BF = ml_dtypes.bfloat16

NCORE = 8
T = 2048
C = 1024
FF = 4096
TT_ALL = NCORE * T
EPS = 1e-5
PPINIT = -40.0
NEGBIG = -3.0e38
POSBIG = 3.0e38
BIS_ITERS = 32
GC1 = 2.0 * 0.7978845608028654
GC2 = 2.0 * 0.7978845608028654 * 0.044715

ENGS = ("sync", "vector", "scalar", "gpsimd", "tensor")
SAFE_SAME_ENGINE = bool(int(os.environ.get("KSAFE", "1")))


def _regs(*aps):
    out = []
    for a in aps:
        if a is None or isinstance(a, (int, float)):
            continue
        t = getattr(a, "tensor", None)
        if t is not None:
            out.append(t.name)
    return out


class Prog:

    def __init__(self, nc):
        self.nc = nc
        self.sem_names = ["sync", "vector", "scalar", "gpsimd", "tensor", "gdma", "ccol"]
        self.counts = {s: 0 for s in self.sem_names}
        self.ops = {e: [] for e in ENGS}
        self.last_w = {}
        self.readers = {}
        self.guar = {e: {s: 0 for s in self.sem_names} for e in ENGS}
        self.floor = {}

    def barrier(self):
        self.floor = dict(self.counts)

    def emit_op(self, stream, fn, reads, writes, is_dma=False, sem=None, inc=None):
        sem = sem or stream
        inc = inc if inc is not None else (16 if is_dma else 1)
        needed = {}

        def add_dep(ev):
            if ev is None:
                return
            (s, sm, c, was_async, snap) = ev
            if s == stream and not was_async and (s == "tensor" or not SAFE_SAME_ENGINE):
                return
            if was_async:
                c = self.counts[sm]
            if self.guar[stream].get(sm, 0) >= c:
                return
            if needed.get(sm, 0) < c:
                needed[sm] = c

        for sm, c in self.floor.items():
            if c > 0 and self.guar[stream].get(sm, 0) < c and needed.get(sm, 0) < c:
                if not (sm == stream == sem and not is_dma):
                    needed[sm] = c
        for rg in reads:
            add_dep(self.last_w.get(rg))
        for rg in writes:
            add_dep(self.last_w.get(rg))
            for ev in self.readers.get(rg, ()):
                add_dep(ev)
        for sm, c in needed.items():
            ev = self._ev_by_sem.get((sm, c))
            if ev is not None:
                snap = ev[4]
                for s2, c2 in snap.items():
                    if self.guar[stream].get(s2, 0) < c2:
                        self.guar[stream][s2] = c2
            if self.guar[stream].get(sm, 0) < c:
                self.guar[stream][sm] = c
        self.counts[sem] += inc
        cnt = self.counts[sem]
        snap = dict(self.guar[stream])
        snap[sem] = cnt
        ev = (stream, sem, cnt, is_dma, snap)
        self._ev_by_sem[(sem, cnt)] = ev
        for rg in writes:
            self.last_w[rg] = ev
            self.readers[rg] = []
        for rg in reads:
            self.readers.setdefault(rg, []).append(ev)
        self.ops[stream].append((fn, dict(needed), sem, inc))

    _ev_by_sem = {}

    def dma(self, out, in_, eng="sync"):
        sem = "gdma" if eng == "gpsimd" else eng
        self.emit_op(eng, lambda e, out=out, in_=in_: e.dma_start(out=out, in_=in_),
                     _regs(in_), _regs(out), is_dma=True, sem=sem)

    def idma_gather(self, out, in_, idxcol):
        self.emit_op("gpsimd", lambda e, out=out, in_=in_, idx=idxcol:
                     e.indirect_dma_start(out, None, in_, bass.IndirectOffsetOnAxis(idx, 0)),
                     _regs(in_, idxcol), _regs(out), is_dma=True, sem="gdma")

    def tt(self, out, a, b, op, eng="vector"):
        self.emit_op(eng, lambda e, out=out, a=a, b=b, op=op: e.tensor_tensor(out, a, b, op),
                     _regs(a, b), _regs(out))

    def ts(self, out, a, s1, s2, op0, op1=None, accum=None, eng="vector"):
        if op1 is None:
            self.emit_op(eng, lambda e, out=out, a=a, s1=s1, op0=op0:
                         e.tensor_scalar(out, a, s1, None, op0),
                         _regs(a, s1), _regs(out))
        else:
            self.emit_op(eng, lambda e, out=out, a=a, s1=s1, s2=s2, op0=op0, op1=op1, ac=accum:
                         e.tensor_scalar(out, a, s1, s2, op0, op1, accum_out=ac),
                         _regs(a, s1, s2), _regs(out, accum))

    def stt(self, out, a, s, b, op0, op1, eng="vector"):
        self.emit_op(eng, lambda e, out=out, a=a, s=s, b=b, op0=op0, op1=op1:
                     e.scalar_tensor_tensor(out, a, s, b, op0, op1),
                     _regs(a, s, b), _regs(out))

    def act(self, out, in_, fn, bias=None, scale=None, accum=None):
        kw = {}
        if bias is not None:
            kw["bias"] = bias
        if scale is not None:
            kw["scale"] = scale
        if accum is not None:
            kw["accum_out"] = accum
        self.emit_op("scalar", lambda e, out=out, in_=in_, fn=fn, kw=kw:
                     e.activation(out, in_, fn, **kw),
                     _regs(in_, bias), _regs(out, accum))

    def copy(self, out, in_, eng="vector"):
        if eng == "scalar":
            self.emit_op(eng, lambda e, out=out, in_=in_:
                         e.activation(out, in_, AF.Copy),
                         _regs(in_), _regs(out))
        else:
            self.emit_op(eng, lambda e, out=out, in_=in_: e.tensor_copy(out, in_),
                         _regs(in_), _regs(out))

    def scan(self, out, d0, d1, init, eng="vector"):
        self.emit_op(eng, lambda e, out=out, d0=d0, d1=d1, init=init:
                     e.tensor_tensor_scan(out, d0, d1, init, OP.mult, OP.add),
                     _regs(d0, d1, init), _regs(out))

    def red(self, out, in_, op=OP.add, eng="vector"):
        self.emit_op(eng, lambda e, out=out, in_=in_, op=op:
                     e.tensor_reduce(out, in_, AXX, op),
                     _regs(in_), _regs(out))

    def mm(self, out, lhsT, rhs, start, stop):
        n = rhs.shape[-1]

        def fn(e, out=out, l=lhsT, r=rhs, st=start, sp=stop, n=n):
            if n <= 512:
                return e.matmul(out, l, r, start=st, stop=sp)
            ret = None
            for o in range(0, n, 512):
                w = min(512, n - o)
                ret = e.matmul(out[:, o:o + w], l, r[:, o:o + w], start=st, stop=sp)
            return ret
        reads = _regs(lhsT, rhs) + ([] if start else _regs(out))
        self.emit_op("tensor", fn, reads, _regs(out))

    def tp(self, out, in_, ident):
        self.emit_op("tensor", lambda e, out=out, in_=in_, id_=ident:
                     e.transpose(out, in_, id_),
                     _regs(in_, ident), _regs(out))

    def memset(self, ap, v, eng="gpsimd"):
        self.emit_op(eng, lambda e, ap=ap, v=v: e.memset(ap, v), [], _regs(ap))

    def iota(self, out, pattern, base, cm):
        self.emit_op("gpsimd", lambda e, out=out, p=pattern, b=base, cm=cm:
                     e.iota(out, pattern=p, base=b, channel_multiplier=cm,
                            allow_small_or_imprecise_dtypes=True),
                     [], _regs(out))

    def cc(self, kind, op, ins, outs):
        self.emit_op("gpsimd", lambda e, kind=kind, op=op, ins=ins, outs=outs:
                     e.collective_compute(kind, op, replica_groups=[list(range(NCORE))],
                                          ins=ins, outs=outs),
                     [t.tensor.name for t in ins], [t.tensor.name for t in outs],
                     is_dma=True, sem="ccol", inc=1)


def build_nc(stage=9, taps=()):
    tapset = set(taps)
    nc = bass.Bass("TRN2", target_bir_lowering=False)

    def din(name, shape, dt=F32):
        return nc.dram_tensor(name, list(shape), dt, kind="ExternalInput")

    x_my = din("x_my", (T, C))
    x_full = din("x_full", (TT_ALL, C))
    wtk = din("wtk", (128, 8, C)); wtv = din("wtv", (128, 8, C)); wtr = din("wtr", (128, 8, C))
    wcr = din("wcr", (128, 8, C))
    wck = din("wck", (32, 128, 8, 128)); wcv = din("wcv", (8, 128, 32, 128))
    w1 = din("w1", (32, 128, 8, 128)); w2 = din("w2", (8, 128, 32, 128))
    b1c_d = din("b1c", (128, 32))
    b2fc_d = din("b2fc", (128, 8))
    tdc_d = din("tdc", (128, 8)); tfc_d = din("tfc", (128, 8))
    mkc_d = din("mkc", (128, 8)); mvc_d = din("mvc", (128, 8)); mrc_d = din("mrc", (128, 8))
    cmkc_d = din("cmkc", (128, 8)); cmrc_d = din("cmrc", (128, 8))
    g1c_d = din("g1c", (128, 8)); b1lc_d = din("b1lc", (128, 8))
    g2c_d = din("g2c", (128, 8)); b2lc_d = din("b2lc", (128, 8))
    gfc_d = din("gfc", (128, 8)); bfc_d = din("bfc", (128, 8))
    centT_d = din("centT", (128, 8, 8))
    ident_d = din("identf", (128, 128))
    identb_d = din("identb", (128, 128), BF16)
    trie_d = din("trie", (128, 128))
    ones_d = din("onesf", (128, 128))
    sel16_d = din("sel16", (128, 16))
    ohc_d = din("ohc", (128, 8))

    out_d = nc.dram_tensor("out", [T, C], F32, kind="ExternalOutput")
    tap_d = {}

    def mktap(name, shape, dt=F32):
        if name in tapset:
            tap_d[name] = nc.dram_tensor("tap_" + name, list(shape), dt, kind="ExternalOutput")

    for nm, shp, dt in [("xn", (T, C), F32), ("scout", (64, T), F32),
                        ("masks", (128, 1024), F32), ("slot", (128, 128), F32),
                        ("idx", (128, 16), I32), ("myslot", (128, 16), I32),
                        ("r", (T, C), F32), ("x1", (T, C), F32), ("kdbg", (C, T), F32),
                        ("vdbg", (C, T), F32), ("ydbg", (C, T), F32), ("ppdbg", (128, 513), F32), ("aadbg", (128, 513), F32), ("dldbg", (128, 512), F32),
                        ("s", (T, C), F32), ("gin", (T, C), F32)]:
        mktap(nm, shp, dt)

    xn_d = nc.dram_tensor("xn_d", [T, C], F32)
    x1_d = nc.dram_tensor("x1_d", [T, C], F32)
    s_dd = nc.dram_tensor("s_dd", [T, C], F32)
    r_d = nc.dram_tensor("r_d", [T, C], F32)
    k_dbg_d = nc.dram_tensor("k_dbg_d", [C, T], F32) if "kdbg" in tapset else None
    v_dbg_d = nc.dram_tensor("v_dbg_d", [C, T], F32) if "vdbg" in tapset else None
    y_dbg_d = nc.dram_tensor("y_dbg_d", [C, T], F32) if "ydbg" in tapset else None
    xng_d = nc.dram_tensor("xng_d", [T, C], F32)
    rowcum_d = nc.dram_tensor("rowcum_d", [128, 128], F32)
    sc_in = nc.dram_tensor("sc_in", [8, T], F32)
    sc_out = nc.dram_tensor("sc_out", [8 * NCORE, T], F32)
    g_in = nc.dram_tensor("g_in", [T, C], BF16)
    g_out = nc.dram_tensor("g_out", [TT_ALL, C], BF16)

    st = ExitStack()
    with st:
        blk = st.enter_context(nc.Block())
        sems = {s: st.enter_context(nc.semaphore("sem_" + s))
                for s in ["sync", "vector", "scalar", "gpsimd", "tensor", "gdma", "ccol"]}
        P = Prog(nc)
        P._ev_by_sem = {}

        def sb(stk, name, shape, dt=F32):
            return stk.enter_context(nc.sbuf_tensor("sb_" + name, list(shape), dt))

        def psum(stk, name, shape, dt=F32):
            return stk.enter_context(nc.psum_tensor("ps_" + name, list(shape), dt))

        tdc = sb(st, "tdc", (128, 8)); tfc = sb(st, "tfc", (128, 8))
        ewc = sb(st, "ewc", (128, 8))
        mkc = sb(st, "mkc", (128, 8)); mvc = sb(st, "mvc", (128, 8)); mrc = sb(st, "mrc", (128, 8))
        cmkc = sb(st, "cmkc", (128, 8)); cmrc = sb(st, "cmrc", (128, 8))
        g1c = sb(st, "g1c", (128, 8)); b1lc = sb(st, "b1lc", (128, 8))
        g2c = sb(st, "g2c", (128, 8)); b2lc = sb(st, "b2lc", (128, 8))
        gfc = sb(st, "gfc", (128, 8)); bfc = sb(st, "bfc", (128, 8))
        b1c = sb(st, "b1c", (128, 32)); b2fc = sb(st, "b2fc", (128, 8))
        identf = sb(st, "identf", (128, 128)); identb = sb(st, "identb", (128, 128), BF16)
        trie = sb(st, "trie", (128, 128)); onesf = sb(st, "onesf", (128, 128))
        sel16 = sb(st, "sel16", (128, 16)); ohc = sb(st, "ohc", (128, 8))
        iot16 = sb(st, "iot16", (128, 16))
        idxcols = sb(st, "idxcols", (128, 16), I32)
        myslots = sb(st, "myslots", (128, 16), I32)
        s1c = sb(st, "s1c", (128, 1)); s2c = sb(st, "s2c", (128, 1))
        muc = sb(st, "muc", (128, 1)); varc = sb(st, "varc", (128, 1)); rsd = sb(st, "rsd", (128, 1))
        tva = sb(st, "tva", (128, 1)); tvb = sb(st, "tvb", (128, 1))
        xt = sb(st, "xt", (128, C)); sq_s = sb(st, "sq_s", (128, C))
        nxt = sb(st, "nxt", (128, C), BF16)
        c_eps = sb(st, "c_eps", (128, 1)); c_one = sb(st, "c_one", (128, 1))
        nx1T = sb(st, "nx1T", (128, 8, 2049), BF16)
        trunk = ExitStack()
        nxnT = sb(trunk, "nxnT", (128, 8, 2049), BF16)

        for tile, d in [(tdc, tdc_d), (tfc, tfc_d), (mkc, mkc_d), (mvc, mvc_d), (mrc, mrc_d),
                        (cmkc, cmkc_d), (cmrc, cmrc_d), (g1c, g1c_d), (b1lc, b1lc_d),
                        (g2c, g2c_d), (b2lc, b2lc_d), (gfc, gfc_d), (bfc, bfc_d),
                        (b1c, b1c_d), (b2fc, b2fc_d), (identf, ident_d), (identb, identb_d),
                        (trie, trie_d), (onesf, ones_d), (sel16, sel16_d), (ohc, ohc_d)]:
            P.dma(tile[:, :], d[:, :])
        P.iota(iot16[:, :], [[128, 16]], 0, 1)
        P.memset(c_eps[:, 0:1], float(EPS))
        P.memset(c_one[:, 0:1], 1.0)
        P.act(ewc[:, :], tdc[:, :], AF.Exp)

        def ln_stats(src, w):
            P.act(sq_s[:, 0:w], src, AF.Copy, accum=s1c[:, 0:1])
            P.tt(sq_s[:, 0:w], src, src, OP.mult)
            P.act(sq_s[:, 0:w], sq_s[:, 0:w], AF.Copy, accum=s2c[:, 0:1])
            P.ts(muc[:, 0:1], s1c[:, 0:1], 1.0 / w, None, OP.mult)
            P.tt(tva[:, 0:1], muc[:, 0:1], muc[:, 0:1], OP.mult)
            P.ts(varc[:, 0:1], s2c[:, 0:1], 1.0 / w, tva[:, 0:1], OP.mult, OP.subtract)
            P.act(tvb[:, 0:1], varc[:, 0:1], AF.Ln, bias=c_eps[:, 0:1])
            P.act(rsd[:, 0:1], tvb[:, 0:1], AF.Exp, scale=-0.5)

        with ExitStack() as p1:
            centTb = sb(p1, "centTb", (128, 8, 8), BF16)
            scev = sb(p1, "scev", (8, T))
            hfT_t = sb(p1, "hfT_t", (128, T), BF16)
            tppsb = psum(p1, "tppsb", (128, 128), BF16)
            scps = psum(p1, "scps", (8, T))
            P.dma(centTb[:, :, :], centT_d[:, :, :], eng="gpsimd")
            for cc in range(8):
                P.memset(nxnT[:, cc, 0:1], 0.0)
            for i in range(16):
                P.dma(xt[:, :], x_my[i * 128:(i + 1) * 128, :])
                ln_stats(xt[:, :], C)
                P.ts(xt[:, :], xt[:, :], muc[:, 0:1], rsd[:, 0:1], OP.subtract, OP.mult)
                P.dma(xn_d[i * 128:(i + 1) * 128, :], xt[:, :], eng="scalar")
                ln_stats(xt[:, :], C)
                P.ts(nxt[:, :], xt[:, :], muc[:, 0:1], rsd[:, 0:1], OP.subtract, OP.mult)
                for cc in range(8):
                    P.tp(tppsb[:, :], nxt[:, cc * 128:(cc + 1) * 128], identb[:, :])
                    P.copy(nxnT[:, cc, 1 + i * 128:1 + (i + 1) * 128], tppsb[:, :])
            for cc in range(8):
                P.ts(hfT_t[:, :], nxnT[:, cc, 1:2049], gfc[:, cc:cc + 1], bfc[:, cc:cc + 1],
                     OP.mult, OP.add)
                for half in range(2):
                    P.mm(scps[0:8, half * 1024:(half + 1) * 1024], centTb[:, cc, :],
                         hfT_t[:, half * 1024:(half + 1) * 1024],
                         start=(cc == 0), stop=(cc == 7))
            P.copy(scev[0:8, :], scps[0:8, :])
            P.dma(sc_in[:, :], scev[0:8, :])
            P.cc("AllGather", OP.bypass, [sc_in.ap().opt()], [sc_out.ap().opt()])
            if "xn" in tapset:
                P.dma(tap_d["xn"][:, :], xn_d[:, :])
            if "scout" in tapset:
                P.dma(tap_d["scout"][:, :], sc_out[:, :])
        P.barrier()

        if stage >= 2:
            with ExitStack() as pr:
                s8 = sb(pr, "s8", (128, 1024))
                takenb = sb(pr, "takenb", (128, 128))
                masksb = sb(pr, "masksb", (128, 1024))
                slotb = sb(pr, "slotb", (128, 128))
                myrc = sb(pr, "myrc", (128, 128))
                myoc = sb(pr, "myoc", (128, 1))
                r_t1 = sb(pr, "r_t1", (128, 128)); r_t2 = sb(pr, "r_t2", (128, 128))
                r_t3 = sb(pr, "r_t3", (128, 128))
                rc1 = sb(pr, "rc1", (128, 1)); rc2 = sb(pr, "rc2", (128, 1))
                rc3 = sb(pr, "rc3", (128, 1)); rc4 = sb(pr, "rc4", (128, 1))
                rc5 = sb(pr, "rc5", (128, 1))
                lob = sb(pr, "lob", (128, 1)); wda = sb(pr, "wda", (128, 1))
                wdb = sb(pr, "wdb", (128, 1)); hib = sb(pr, "hib", (128, 1))
                lo0b = sb(pr, "lo0b", (128, 1)); hi0b = sb(pr, "hi0b", (128, 1))
                obb = sb(pr, "obb", (128, 128)); orow = sb(pr, "orow", (128, 128))
                pjc = sb(pr, "pjc", (128, 1)); ljc = sb(pr, "ljc", (128, 1))
                posc = sb(pr, "posc", (128, 1)); pji = sb(pr, "pji", (128, 1), I32)
                RG = sb(pr, "RG", (128, 128))
                cntps = psum(pr, "cntps", (128, 1))
                tppsf = psum(pr, "tppsf", (128, 128))
                selps = psum(pr, "selps", (16, 128))

                for e in range(8):
                    ap_in = bass.AP(sc_out, e * T, [[8 * T, 8], [128, 16], [1, 128]])
                    P.dma(s8[:, e * 128:(e + 1) * 128], ap_in)

                def part_reduce(vec_col, op, neutral, dst_col):
                    P.memset(r_t2[:, :], neutral)
                    P.copy(r_t2[:, 0:1], vec_col)
                    P.tp(tppsf[:, :], r_t2[:, :], identf[:, :])
                    P.copy(r_t3[:, :], tppsf[:, :])
                    P.red(rc2[0:1, 0:1], r_t3[0:1, 0:128], op)
                    P.mm(cntps[:, 0:1], onesf[0:1, :], rc2[0:1, 0:1], start=True, stop=True)
                    P.copy(dst_col, cntps[:, 0:1])

                P.red(rc1[:, 0:1], s8[:, :], OP.max)
                part_reduce(rc1[:, 0:1], OP.max, NEGBIG, hi0b[:, 0:1])
                P.ts(hi0b[:, 0:1], hi0b[:, 0:1], 1.0, None, OP.add)
                P.red(rc1[:, 0:1], s8[:, :], OP.min)
                part_reduce(rc1[:, 0:1], OP.min, POSBIG, lo0b[:, 0:1])
                P.ts(lo0b[:, 0:1], lo0b[:, 0:1], -1.0, None, OP.add)

                P.memset(takenb[:, :], 0.0)
                for e in range(8):
                    if e == 7:
                        P.ts(masksb[:, 7 * 128:8 * 128], takenb[:, :], -1.0, 1.0,
                             OP.mult, OP.add)
                        break
                    P.stt(r_t1[:, :], takenb[:, :], NEGBIG, s8[:, e * 128:(e + 1) * 128],
                          OP.mult, OP.add)
                    P.copy(lob[:, 0:1], lo0b[:, 0:1])
                    P.tt(wda[:, 0:1], hi0b[:, 0:1], lo0b[:, 0:1], OP.subtract)
                    wcur, wnext = wda, wdb
                    for it in range(BIS_ITERS):
                        P.ts(wnext[:, 0:1], wcur[:, 0:1], 0.5, None, OP.mult)
                        P.tt(rc3[:, 0:1], lob[:, 0:1], wnext[:, 0:1], OP.add)
                        P.ts(r_t2[:, :], r_t1[:, :], rc3[:, 0:1], None, OP.is_gt, OP.add,
                             accum=rc4[:, 0:1])
                        P.mm(cntps[:, 0:1], onesf[:, :], rc4[:, 0:1], start=True, stop=True)
                        P.ts(rc5[:, 0:1], cntps[:, 0:1], float(T), None, OP.is_ge)
                        P.tt(rc4[:, 0:1], rc5[:, 0:1], wnext[:, 0:1], OP.mult)
                        P.tt(lob[:, 0:1], lob[:, 0:1], rc4[:, 0:1], OP.add)
                        wcur, wnext = wnext, wcur
                    P.tt(hib[:, 0:1], lob[:, 0:1], wcur[:, 0:1], OP.add)
                    P.ts(r_t2[:, :], r_t1[:, :], hib[:, 0:1], None, OP.is_gt, OP.add,
                         accum=rc4[:, 0:1])
                    P.mm(cntps[:, 0:1], onesf[:, :], rc4[:, 0:1], start=True, stop=True)
                    P.ts(rc5[:, 0:1], cntps[:, 0:1], -1.0, float(T), OP.mult, OP.add)
                    P.ts(r_t3[:, :], r_t1[:, :], lob[:, 0:1], None, OP.is_gt)
                    P.tt(r_t3[:, :], r_t3[:, :], r_t2[:, :], OP.subtract)
                    P.scan(r_t1[:, :], onesf[:, :], r_t3[:, :], 0.0)
                    P.mm(cntps[:, 0:1], trie[:, :], r_t1[:, 127:128], start=True, stop=True)
                    P.copy(rc1[:, 0:1], cntps[:, 0:1])
                    P.ts(r_t1[:, :], r_t1[:, :], rc1[:, 0:1], None, OP.add)
                    P.ts(r_t1[:, :], r_t1[:, :], rc5[:, 0:1], None, OP.is_le)
                    P.tt(r_t3[:, :], r_t3[:, :], r_t1[:, :], OP.mult)
                    P.tt(masksb[:, e * 128:(e + 1) * 128], r_t2[:, :], r_t3[:, :], OP.add)
                    P.tt(takenb[:, :], takenb[:, :], masksb[:, e * 128:(e + 1) * 128], OP.add)

                P.memset(slotb[:, :], 0.0)
                P.memset(myrc[:, :], 0.0)
                P.memset(myoc[:, 0:1], 0.0)
                for e in range(8):
                    me = masksb[:, e * 128:(e + 1) * 128]
                    P.scan(r_t2[:, :], onesf[:, :], me, 0.0)
                    P.mm(cntps[:, 0:1], trie[:, :], r_t2[:, 127:128], start=True, stop=True)
                    P.copy(rc1[:, 0:1], cntps[:, 0:1])
                    P.ts(r_t3[:, :], r_t2[:, :], rc1[:, 0:1], None, OP.add)
                    P.ts(r_t3[:, :], r_t3[:, :], float(e * T - 1), None, OP.add)
                    P.tt(r_t3[:, :], r_t3[:, :], me, OP.mult)
                    P.tt(slotb[:, :], slotb[:, :], r_t3[:, :], OP.add)
                    P.ts(r_t3[:, :], r_t2[:, :], ohc[:, e:e + 1], None, OP.mult)
                    P.tt(myrc[:, :], myrc[:, :], r_t3[:, :], OP.add)
                    P.ts(rc2[:, 0:1], rc1[:, 0:1], ohc[:, e:e + 1], None, OP.mult)
                    P.tt(myoc[:, 0:1], myoc[:, 0:1], rc2[:, 0:1], OP.add)

                P.mm(selps[0:16, :], sel16[:, :], slotb[:, :], start=True, stop=True)
                P.copy(r_t2[0:16, :], selps[0:16, :])
                P.tp(tppsf[:, :], r_t2[:, :], identf[:, :])
                P.copy(r_t3[:, :], tppsf[:, :])
                P.copy(myslots[:, :], r_t3[:, 0:16])

                P.dma(rowcum_d[:, :], myrc[:, :])
                P.memset(r_t2[:, :], 0.0)
                P.copy(r_t2[:, 0:1], myoc[:, 0:1])
                P.tp(tppsf[:, :], r_t2[:, :], identf[:, :])
                P.copy(orow[:, :], tppsf[:, :])
                P.mm(tppsf[:, :], onesf[0:1, :], orow[0:1, :], start=True, stop=True)
                P.copy(obb[:, :], tppsf[:, :])
                for i in range(16):
                    jcol = iot16[:, i:i + 1]
                    P.ts(r_t2[:, :], obb[:, :], jcol, None, OP.is_le, OP.add,
                         accum=pjc[:, 0:1])
                    P.ts(pjc[:, 0:1], pjc[:, 0:1], -1.0, None, OP.add)
                    P.tt(r_t3[:, :], obb[:, :], r_t2[:, :], OP.mult)
                    P.red(rc1[:, 0:1], r_t3[:, :], OP.max)
                    P.tt(ljc[:, 0:1], jcol, rc1[:, 0:1], OP.subtract)
                    P.copy(pji[:, 0:1], pjc[:, 0:1])
                    P.idma_gather(RG[:, :], rowcum_d[0:128, 0:128], pji[0:128, 0:1])
                    P.ts(r_t2[:, :], RG[:, :], ljc[:, 0:1], None, OP.is_le, OP.add,
                         accum=posc[:, 0:1])
                    P.ts(rc1[:, 0:1], pjc[:, 0:1], 128.0, posc[:, 0:1], OP.mult, OP.add)
                    P.copy(idxcols[:, i:i + 1], rc1[:, 0:1])

                if "masks" in tapset:
                    P.dma(tap_d["masks"][:, :], masksb[:, :])
                if "slot" in tapset:
                    P.dma(tap_d["slot"][:, :], slotb[:, :])
                if "idx" in tapset:
                    P.dma(tap_d["idx"][:, :], idxcols[:, :])
                if "myslot" in tapset:
                    P.dma(tap_d["myslot"][:, :], myslots[:, :])
            P.barrier()

        if stage >= 3:
            with ExitStack() as p4:
                wtk_s = sb(p4, "wtk_s", (128, 8, C), BF16)
                wtv_s = sb(p4, "wtv_s", (128, 8, C), BF16)
                mixd = sb(p4, "mixd", (128, 512), BF16)
                xkTg = sb(p4, "xkTg", (128, 8, 512), BF16)
                xvTg = sb(p4, "xvTg", (128, 8, 512), BF16)
                kbuf = sb(p4, "kbuf", (128, 512))
                vbuf = sb(p4, "vbuf", (128, 512))
                yTtg = sb(p4, "yTtg", (128, 8, 512))
                ones512 = sb(p4, "ones512", (128, 512))
                ppt0 = sb(p4, "ppt0", (128, 513))
                aat0 = sb(p4, "aat0", (128, 513))
                ppcar = sb(p4, "ppcar", (128, 8)); aacar = sb(p4, "aacar", (128, 8))
                wpc = sb(p4, "wpc", (128, 1))
                sc1 = sb(p4, "sc1", (128, 512)); sc2 = sb(p4, "sc2", (128, 512))
                sc3 = sb(p4, "sc3", (128, 512)); sc4 = sb(p4, "sc4", (128, 512))
                sc5 = sb(p4, "sc5", (128, 512)); sc6 = sb(p4, "sc6", (128, 512))
                yt_f = sb(p4, "yt_f", (128, C)); rt_f = sb(p4, "rt_f", (128, C))
                x1t = sb(p4, "x1t", (128, C))
                mmps_a = psum(p4, "mmps_a", (128, 512))
                mmps_b = psum(p4, "mmps_b", (128, 512))
                mmps_w = psum(p4, "mmps_w", (128, 1024))
                tppsf4 = psum(p4, "tppsf4", (128, 128))
                tppsb2 = psum(p4, "tppsb2", (128, 128), BF16)

                for wsb, wd in [(wtk_s, wtk), (wtv_s, wtv)]:
                    P.dma(wsb[:, :, :], wd[:, :, :], eng="gpsimd")
                P.memset(ones512[:, :], 1.0)
                P.memset(ppcar[:, :], 0.0)
                P.memset(aacar[:, :], 0.0)

                for cc in range(8):
                    P.ts(nxnT[:, cc, 1:2049], nxnT[:, cc, 1:2049], g1c[:, cc:cc + 1],
                         b1lc[:, cc:cc + 1], OP.mult, OP.add)
                hT = nxnT

                with ExitStack() as p3:
                    wtr_s = sb(p3, "wtr_s", (128, 8, C), BF16)
                    xr_t = sb(p3, "xr_t", (128, 8, 128), BF16)
                    P.dma(wtr_s[:, :, :], wtr[:, :, :], eng="gpsimd")
                    for ti in range(16):
                        for cc in range(8):
                            hp = hT[:, cc, ti * 128:(ti + 1) * 128]
                            hh = hT[:, cc, 1 + ti * 128:1 + (ti + 1) * 128]
                            P.tt(xr_t[:, cc, :], hh, hp, OP.subtract)
                            P.stt(xr_t[:, cc, :], xr_t[:, cc, :], mrc[:, cc:cc + 1], hp,
                                  OP.mult, OP.add)
                        for dh in range(8):
                            P.mm(mmps_w[:, :], xr_t[:, dh, :], wtr_s[:, dh, :],
                                 start=(dh == 0), stop=(dh == 7))
                        P.act(rt_f[:, :], mmps_w[:, :], AF.Sigmoid)
                        P.dma(r_d[ti * 128:(ti + 1) * 128, :], rt_f[:, :])
                if "r" in tapset:
                    P.dma(tap_d["r"][:, :], r_d[:, :])

                for cc in range(8):
                    P.memset(nx1T[:, cc, 0:1], 0.0)
                for tg in range(4):
                    t0 = tg * 512
                    for cc in range(8):
                        hp = hT[:, cc, t0:t0 + 512]
                        hh = hT[:, cc, 1 + t0:1 + t0 + 512]
                        P.tt(mixd[:, :], hh, hp, OP.subtract)
                        P.stt(xkTg[:, cc, :], mixd[:, :], mkc[:, cc:cc + 1], hp,
                              OP.mult, OP.add)
                        P.stt(xvTg[:, cc, :], mixd[:, :], mvc[:, cc:cc + 1], hp,
                              OP.mult, OP.add)
                    for cc in range(8):
                        for dh in range(8):
                            P.mm(mmps_a[:, :], wtk_s[:, dh, cc * 128:(cc + 1) * 128],
                                 xkTg[:, dh, :], start=(dh == 0), stop=(dh == 7))
                        P.copy(kbuf[:, :], mmps_a[:, :], eng="scalar")
                        for dh in range(8):
                            P.mm(mmps_b[:, :], wtv_s[:, dh, cc * 128:(cc + 1) * 128],
                                 xvTg[:, dh, :], start=(dh == 0), stop=(dh == 7))
                        P.copy(vbuf[:, :], mmps_b[:, :], eng="scalar")
                        if k_dbg_d is not None:
                            P.dma(k_dbg_d[cc * 128:(cc + 1) * 128, t0:t0 + 512], kbuf[:, :])
                        if v_dbg_d is not None:
                            P.dma(v_dbg_d[cc * 128:(cc + 1) * 128, t0:t0 + 512], vbuf[:, :])
                        kk_ = kbuf[:, :]; vv_ = vbuf[:, :]
                        pc = ppcar[:, cc:cc + 1]; ac = aacar[:, cc:cc + 1]
                        pt = ppt0; at = aat0
                        P.ts(sc6[:, :], ones512[:, :], ewc[:, cc:cc + 1], None, OP.mult)
                        P.ts(wpc[:, 0:1], tdc[:, cc:cc + 1], pc, None, OP.subtract)
                        P.ts(sc1[:, :], kk_, wpc[:, 0:1], None, OP.add)
                        P.act(sc2[:, :], sc1[:, :], AF.Exp)
                        P.scan(sc3[:, :], sc6[:, :], sc2[:, :], 0.0 if tg == 0 else 1.0)
                        if cc == 0 and tg == 0 and "dldbg" in tapset:
                            P.dma(tap_d["dldbg"][:, :], sc3[:, :])
                        P.act(sc4[:, :], sc3[:, :], AF.Ln)
                        if tg == 0:
                            P.memset(pt[:, 0:1], PPINIT)
                        else:
                            P.copy(pt[:, 0:1], pc)
                        P.ts(pt[:, 1:513], sc4[:, :], pc, None, OP.add)
                        P.copy(ppcar[:, cc:cc + 1], pt[:, 512:513])
                        P.tt(sc1[:, :], pt[:, 0:512], kk_, OP.subtract)
                        P.ts(sc2[:, :], sc1[:, :], 0.0, None, OP.min)
                        P.act(sc3[:, :], sc2[:, :], AF.Exp)
                        P.ts(sc2[:, :], sc1[:, :], 0.0, None, OP.max)
                        P.act(sc4[:, :], sc2[:, :], AF.Exp, scale=-1.0)
                        P.tt(sc5[:, :], sc4[:, :], vv_, OP.mult, eng="gpsimd")
                        P.copy(at[:, 0:1], ac)
                        P.scan(at[:, 1:513], sc3[:, :], sc5[:, :], ac)
                        P.copy(aacar[:, cc:cc + 1], at[:, 512:513])
                        P.ts(sc1[:, :], sc1[:, :], tfc[:, cc:cc + 1], None, OP.subtract)
                        P.ts(sc2[:, :], sc1[:, :], 0.0, None, OP.min)
                        P.act(sc3[:, :], sc2[:, :], AF.Exp)
                        P.ts(sc2[:, :], sc1[:, :], 0.0, None, OP.max)
                        P.act(sc4[:, :], sc2[:, :], AF.Exp, scale=-1.0)
                        P.tt(sc5[:, :], sc3[:, :], at[:, 0:512], OP.mult)
                        P.tt(sc2[:, :], sc4[:, :], vv_, OP.mult, eng="gpsimd")
                        P.tt(sc5[:, :], sc5[:, :], sc2[:, :], OP.add)
                        P.tt(sc2[:, :], sc3[:, :], sc4[:, :], OP.mult, eng="gpsimd")
                        P.act(sc3[:, :], sc2[:, :], AF.Ln, bias=c_one[:, 0:1])
                        P.act(sc2[:, :], sc3[:, :], AF.Exp, scale=-1.0)
                        P.tt(yTtg[:, cc, :], sc5[:, :], sc2[:, :], OP.mult)
                        if y_dbg_d is not None:
                            P.dma(y_dbg_d[cc * 128:(cc + 1) * 128, t0:t0 + 512],
                                  yTtg[:, cc, :])
                        if cc == 0 and tg == 0 and "ppdbg" in tapset:
                            P.dma(tap_d["ppdbg"][:, :], pt[:, :])
                            P.dma(tap_d["aadbg"][:, :], at[:, :])
                    for ti in range(4):
                        tt_ = tg * 4 + ti
                        for cc in range(8):
                            P.tp(tppsf4[:, :], yTtg[:, cc, ti * 128:(ti + 1) * 128],
                                 identf[:, :])
                            P.copy(yt_f[:, cc * 128:(cc + 1) * 128], tppsf4[:, :])
                        P.dma(rt_f[:, :], r_d[tt_ * 128:(tt_ + 1) * 128, :], eng="scalar")
                        P.dma(xt[:, :], xn_d[tt_ * 128:(tt_ + 1) * 128, :], eng="scalar")
                        P.tt(x1t[:, :], rt_f[:, :], yt_f[:, :], OP.mult)
                        P.tt(x1t[:, :], x1t[:, :], xt[:, :], OP.add)
                        P.dma(x1_d[tt_ * 128:(tt_ + 1) * 128, :], x1t[:, :], eng="scalar")
                        ln_stats(x1t[:, :], C)
                        P.ts(nxt[:, :], x1t[:, :], muc[:, 0:1], rsd[:, 0:1],
                             OP.subtract, OP.mult)
                        for cc in range(8):
                            P.tp(tppsb2[:, :], nxt[:, cc * 128:(cc + 1) * 128], identb[:, :])
                            P.copy(nx1T[:, cc, 1 + tt_ * 128:1 + (tt_ + 1) * 128],
                                   tppsb2[:, :])
                if "x1" in tapset:
                    P.dma(tap_d["x1"][:, :], x1_d[:, :])
                if k_dbg_d is not None:
                    P.dma(tap_d["kdbg"][:, :], k_dbg_d[:, :])
                if v_dbg_d is not None:
                    P.dma(tap_d["vdbg"][:, :], v_dbg_d[:, :])
                if y_dbg_d is not None:
                    P.dma(tap_d["ydbg"][:, :], y_dbg_d[:, :])

            P.barrier()
        trunk.close()

        if stage >= 4:
            with ExitStack() as p5:
                wcr_s = sb(p5, "wcr_s", (128, 8, C), BF16)
                ckTg = sb(p5, "ckTg", (128, 8, 1024), BF16)
                crTg = sb(p5, "crTg", (128, 8, 1024), BF16)
                kkT = sb(p5, "kkT", (128, 32, 1024), BF16)
                cvstage = sb(p5, "cvstage", (128, 8, 1024))
                wstr = [sb(p5, "wstr_a", (128, 8, 128), BF16),
                        sb(p5, "wstr_b", (128, 8, 128), BF16)]
                wstr2 = [sb(p5, "wstr2_a", (128, 16, 128), BF16),
                         sb(p5, "wstr2_b", (128, 16, 128), BF16)]
                dscr = nxt
                ev2 = sb(p5, "ev2", (128, 1024))
                ev1l = [sq_s, ev2]
                ev1 = sq_s
                pwl = [psum(p5, "pw_a", (128, 1024)), psum(p5, "pw_b", (128, 1024))]
                pw2 = psum(p5, "pw2", (128, 1024))
                tppsf5l = [psum(p5, "tppsf5a", (128, 128)), psum(p5, "tppsf5b", (128, 128))]

                P.dma(wcr_s[:, :, :], wcr[:, :, :], eng="gpsimd")
                for cc in range(8):
                    P.ts(nx1T[:, cc, 1:2049], nx1T[:, cc, 1:2049], g2c[:, cc:cc + 1],
                         b2lc[:, cc:cc + 1], OP.mult, OP.add)
                for tg in range(2):
                    t0 = tg * 1024
                    for cc in range(8):
                        hp = nx1T[:, cc, t0:t0 + 1024]
                        hh = nx1T[:, cc, 1 + t0:1 + t0 + 1024]
                        P.tt(dscr[:, :], hh, hp, OP.subtract)
                        P.stt(ckTg[:, cc, :], dscr[:, :], cmkc[:, cc:cc + 1], hp,
                              OP.mult, OP.add)
                        P.stt(crTg[:, cc, :], dscr[:, :], cmrc[:, cc:cc + 1], hp,
                              OP.mult, OP.add)
                    for f in range(32):
                        wb = wstr[f % 2]
                        pw = pwl[f % 2]
                        P.dma(wb[:, :, :], wck[f], eng="gpsimd")
                        for dh in range(8):
                            P.mm(pw[:, :], wb[:, dh, :], ckTg[:, dh, :],
                                 start=(dh == 0), stop=(dh == 7))
                        ev1c = ev1l[f % 2]
                        P.ts(ev1c[:, :], pw[:, :], 0.0, None, OP.max)
                        P.tt(kkT[:, f, :], ev1c[:, :], ev1c[:, :], OP.mult)
                    for c in range(8):
                        pw = pwl[c % 2]
                        P.dma(wstr2[0][:, :, :], wcv.ap()[c, :, 0:16, :], eng="gpsimd")
                        P.dma(wstr2[1][:, :, :], wcv.ap()[c, :, 16:32, :], eng="gpsimd")
                        for f in range(32):
                            P.mm(pw[:, :], wstr2[f // 16][:, f % 16, :], kkT[:, f, :],
                                 start=(f == 0), stop=(f == 31))
                        P.copy(ev1[:, :], pw[:, :], eng="scalar")
                        for ti in range(8):
                            tp5 = tppsf5l[ti % 2]
                            P.tp(tp5[:, :], ev1[:, ti * 128:(ti + 1) * 128], identf[:, :])
                            P.copy(cvstage[:, ti, c * 128:(c + 1) * 128], tp5[:, :])
                    for ti in range(8):
                        tt_ = tg * 8 + ti
                        for dh in range(8):
                            P.mm(pw2[:, :], crTg[:, dh, ti * 128:(ti + 1) * 128],
                                 wcr_s[:, dh, :], start=(dh == 0), stop=(dh == 7))
                        P.act(ev2[:, :], pw2[:, :], AF.Sigmoid)
                        P.tt(ev2[:, :], ev2[:, :], cvstage[:, ti, :], OP.mult)
                        P.dma(xt[:, :], x1_d[tt_ * 128:(tt_ + 1) * 128, :], eng="scalar")
                        P.tt(ev2[:, :], ev2[:, :], xt[:, :], OP.add)
                        P.dma(s_dd[tt_ * 128:(tt_ + 1) * 128, :], ev2[:, :], eng="scalar")
                if "s" in tapset:
                    P.dma(tap_d["s"][:, :], s_dd[:, :])
            P.barrier()

        if stage >= 5:
            with ExitStack() as p6:
                hgT = sb(p6, "hgT", (128, 8, T), BF16)
                h1T = sb(p6, "h1T", (128, 32, 1024), BF16)
                w2stage = sb(p6, "w2stage", (128, 8, 1024))
                wstr6 = [sb(p6, "wstr6_a", (128, 8, 128), BF16),
                         sb(p6, "wstr6_b", (128, 8, 128), BF16)]
                wstr26 = [sb(p6, "wstr26_a", (128, 16, 128), BF16),
                          sb(p6, "wstr26_b", (128, 16, 128), BF16)]
                ev26 = sb(p6, "ev26", (128, 1024)); ev36 = sb(p6, "ev36", (128, 1024))
                gbf6 = sb(p6, "gbf6", (128, 1024), BF16)
                ev16 = sq_s
                xg6 = xt
                pgl = [psum(p6, "pg_a", (128, 1024)), psum(p6, "pg_b", (128, 1024))]
                tppsf6l = [psum(p6, "tppsf6a", (128, 128)), psum(p6, "tppsf6b", (128, 128))]
                tppsb6 = psum(p6, "tppsb6", (128, 128), BF16)

                for ti in range(16):
                    P.idma_gather(xg6[:, :], x_full[0:128, 0:C], idxcols[0:128, ti:ti + 1])
                    ln_stats(xg6[:, :], C)
                    P.ts(xg6[:, :], xg6[:, :], muc[:, 0:1], rsd[:, 0:1],
                         OP.subtract, OP.mult)
                    ln_stats(xg6[:, :], C)
                    P.ts(nxt[:, :], xg6[:, :], muc[:, 0:1], rsd[:, 0:1],
                         OP.subtract, OP.mult)
                    for cc in range(8):
                        P.tp(tppsb6[:, :], nxt[:, cc * 128:(cc + 1) * 128], identb[:, :])
                        P.copy(hgT[:, cc, ti * 128:(ti + 1) * 128], tppsb6[:, :])
                for cc in range(8):
                    P.ts(hgT[:, cc, :], hgT[:, cc, :], gfc[:, cc:cc + 1], bfc[:, cc:cc + 1],
                         OP.mult, OP.add)

                for tg in range(2):
                    t0 = tg * 1024
                    for f in range(32):
                        wb6 = wstr6[f % 2]
                        pg = pgl[f % 2]
                        P.dma(wb6[:, :, :], w1[f], eng="gpsimd")
                        for dh in range(8):
                            P.mm(pg[:, :], wb6[:, dh, :], hgT[:, dh, t0:t0 + 1024],
                                 start=(dh == 0), stop=(dh == 7))
                        ev16c = [ev16, xg6][f % 2]
                        P.ts(ev16c[:, :], pg[:, :], b1c[:, f:f + 1], None, OP.add)
                        P.tt(ev26[:, :], ev16c[:, :], ev16c[:, :], OP.mult)
                        P.ts(ev26[:, :], ev26[:, :], GC2, GC1, OP.mult, OP.add)
                        P.tt(ev36[:, :], ev16c[:, :], ev26[:, :], OP.mult)
                        P.act(ev36[:, :], ev36[:, :], AF.Sigmoid)
                        P.tt(h1T[:, f, :], ev16c[:, :], ev36[:, :], OP.mult)
                    for c in range(8):
                        pg = pgl[c % 2]
                        P.dma(wstr26[0][:, :, :], w2.ap()[c, :, 0:16, :], eng="gpsimd")
                        P.dma(wstr26[1][:, :, :], w2.ap()[c, :, 16:32, :], eng="gpsimd")
                        for f in range(32):
                            P.mm(pg[:, :], wstr26[f // 16][:, f % 16, :], h1T[:, f, :],
                                 start=(f == 0), stop=(f == 31))
                        P.ts(ev16[:, :], pg[:, :], b2fc[:, c:c + 1], None, OP.add)
                        for ti in range(8):
                            tp6 = tppsf6l[ti % 2]
                            P.tp(tp6[:, :], ev16[:, ti * 128:(ti + 1) * 128], identf[:, :])
                            P.copy(w2stage[:, ti, c * 128:(c + 1) * 128], tp6[:, :])
                    for ti in range(8):
                        tt_ = tg * 8 + ti
                        P.copy(gbf6[:, :], w2stage[:, ti, :])
                        P.dma(g_in[tt_ * 128:(tt_ + 1) * 128, :], gbf6[:, :])
                if "gin" in tapset:
                    P.dma(tap_d["gin"][:, :], g_in[:, :])

                P.cc("AllGather", OP.bypass, [g_in.ap().opt()], [g_out.ap().opt()])

                for ti in range(16):
                    P.idma_gather(gbf6[:, :], g_out[0:128, 0:C], myslots[0:128, ti:ti + 1])
                    P.dma(ev36[:, :], xn_d[ti * 128:(ti + 1) * 128, :])
                    P.tt(ev16[:, :], gbf6[:, :], ev36[:, :], OP.add)
                    P.act(ev26[:, :], ev16[:, :], AF.Sigmoid)
                    P.tt(ev26[:, :], ev16[:, :], ev26[:, :], OP.mult)
                    P.dma(xt[:, :], x_my[ti * 128:(ti + 1) * 128, :])
                    P.dma(ev36[:, :], s_dd[ti * 128:(ti + 1) * 128, :], eng="scalar")
                    P.tt(ev26[:, :], ev36[:, :], ev26[:, :], OP.mult)
                    P.tt(ev26[:, :], ev26[:, :], xt[:, :], OP.add)
                    P.dma(out_d[ti * 128:(ti + 1) * 128, :], ev26[:, :], eng="scalar")

        totals = dict(P.counts)
        for engname in ENGS:
            lst = P.ops[engname]
            if not lst:
                continue

            def make(lst):
                def _(h):
                    for fn, waits, semname, inc in lst:
                        for sm, v in waits.items():
                            h.wait_ge(sems[sm], v)
                        r = fn(h)
                        if inc:
                            r.then_inc(sems[semname], inc)
                return _
            getattr(blk, engname)(make(lst))

        @blk.sync
        def _(h):
            for sm, v in totals.items():
                if v > 0:
                    h.wait_ge(sems[sm], v)

    return nc


def _tile_sq(w):
    w = np.asarray(w, np.float32)
    return np.ascontiguousarray(w.reshape(8, 128, 1024).transpose(1, 0, 2))


def _tile_kf(w):
    w = np.asarray(w, np.float32)
    return np.ascontiguousarray(w.reshape(8, 128, 32, 128).transpose(2, 1, 0, 3))


def _tile_fk(w):
    w = np.asarray(w, np.float32)
    return np.ascontiguousarray(w.reshape(32, 128, 8, 128).transpose(2, 1, 0, 3))


def make_inputs(inputs, b):
    f32 = np.float32
    x = np.ascontiguousarray(np.asarray(inputs["L0_attn_x"], f32))
    col8 = lambda v: np.ascontiguousarray(np.asarray(v, f32).reshape(8, 128).T)
    m = {
        "x_my": np.ascontiguousarray(x[b]),
        "x_full": x.reshape(TT_ALL, C),
        "wtk": _tile_sq(inputs["L0_attn_w_tk"]),
        "wtv": _tile_sq(inputs["L0_attn_w_tv"]),
        "wtr": _tile_sq(inputs["L0_attn_w_tr"]),
        "wcr": _tile_sq(inputs["L0_attn_w_cr"]),
        "wck": _tile_kf(inputs["L0_attn_w_ck"]),
        "wcv": _tile_fk(inputs["L0_attn_w_cv"]),
        "w1": _tile_kf(np.asarray(inputs["L0_ffn_w1"], np.float32)[b]),
        "w2": _tile_fk(np.asarray(inputs["L0_ffn_w2"], np.float32)[b]),
        "b1c": np.ascontiguousarray(np.asarray(inputs["L0_ffn_b1"], f32)[b].reshape(32, 128).T),
        "b2fc": col8(np.asarray(inputs["L0_ffn_b2"], f32)[b]),
        "tdc": col8(inputs["L0_attn_time_decay"]),
        "tfc": col8(inputs["L0_attn_time_first"]),
        "mkc": col8(inputs["L0_attn_mix_k"]),
        "mvc": col8(inputs["L0_attn_mix_v"]),
        "mrc": col8(inputs["L0_attn_mix_r"]),
        "cmkc": col8(inputs["L0_attn_cmix_k"]),
        "cmrc": col8(inputs["L0_attn_cmix_r"]),
        "g1c": col8(inputs["L0_attn_ln1_g"]),
        "b1lc": col8(inputs["L0_attn_ln1_b"]),
        "g2c": col8(inputs["L0_attn_ln2_g"]),
        "b2lc": col8(inputs["L0_attn_ln2_b"]),
        "gfc": col8(inputs["L0_ffn_ln_g"]),
        "bfc": col8(inputs["L0_ffn_ln_b"]),
        "centT": np.ascontiguousarray(
            np.asarray(inputs["L0_ffn_centroids"], f32).T.reshape(8, 128, 8)
            .transpose(1, 0, 2)),
        "identf": np.eye(128, dtype=f32),
        "identb": np.eye(128, dtype=f32).astype(BF),
        "trie": np.triu(np.ones((128, 128), f32), 1),
        "onesf": np.ones((128, 128), f32),
        "sel16": None,
        "ohc": None,
    }
    sel = np.zeros((128, 16), f32)
    for i in range(16):
        sel[16 * b + i, i] = 1.0
    m["sel16"] = sel
    oh = np.zeros((128, 8), f32)
    oh[:, b] = 1.0
    m["ohc"] = oh
    return m


def kernel(**inputs):
    nc = build_nc(stage=9)
    in_maps = [make_inputs(inputs, b) for b in range(NCORE)]
    res = run_bass_kernel_spmd(nc, in_maps, core_ids=list(range(NCORE)))
    out = np.stack([np.asarray(res.results[b]["out"]) for b in range(NCORE)])
    return out.astype(np.float32)


# version 23
# speedup vs baseline: 1.0090x; 1.0090x over previous
import os
import numpy as np
import ml_dtypes
from contextlib import ExitStack

import concourse.bass as bass
import concourse.mybir as mybir
from concourse.bass_utils import run_bass_kernel_spmd

F32 = mybir.dt.float32
BF16 = mybir.dt.bfloat16
I32 = mybir.dt.int32
AF = mybir.ActivationFunctionType
OP = mybir.AluOpType
AXX = mybir.AxisListType.X
BF = ml_dtypes.bfloat16

NCORE = 8
T = 2048
C = 1024
FF = 4096
TT_ALL = NCORE * T
EPS = 1e-5
PPINIT = -40.0
NEGBIG = -3.0e38
POSBIG = 3.0e38
BIS_ITERS = 32
GC1 = 2.0 * 0.7978845608028654
GC2 = 2.0 * 0.7978845608028654 * 0.044715

ENGS = ("sync", "vector", "scalar", "gpsimd", "tensor")
SAFE_SAME_ENGINE = bool(int(os.environ.get("KSAFE", "1")))


def _regs(*aps):
    out = []
    for a in aps:
        if a is None or isinstance(a, (int, float)):
            continue
        t = getattr(a, "tensor", None)
        if t is not None:
            out.append(t.name)
    return out


class Prog:

    def __init__(self, nc):
        self.nc = nc
        self.sem_names = ["sync", "vector", "scalar", "gpsimd", "tensor", "gdma", "ccol"]
        self.counts = {s: 0 for s in self.sem_names}
        self.ops = {e: [] for e in ENGS}
        self.last_w = {}
        self.readers = {}
        self.guar = {e: {s: 0 for s in self.sem_names} for e in ENGS}
        self.floor = {}

    def barrier(self):
        self.floor = dict(self.counts)

    def emit_op(self, stream, fn, reads, writes, is_dma=False, sem=None, inc=None):
        sem = sem or stream
        inc = inc if inc is not None else (16 if is_dma else 1)
        needed = {}

        def add_dep(ev):
            if ev is None:
                return
            (s, sm, c, was_async, snap) = ev
            if s == stream and not was_async and (s == "tensor" or not SAFE_SAME_ENGINE):
                return
            if was_async:
                c = self.counts[sm]
            if self.guar[stream].get(sm, 0) >= c:
                return
            if needed.get(sm, 0) < c:
                needed[sm] = c

        for sm, c in self.floor.items():
            if c > 0 and self.guar[stream].get(sm, 0) < c and needed.get(sm, 0) < c:
                if not (sm == stream == sem and not is_dma):
                    needed[sm] = c
        for rg in reads:
            add_dep(self.last_w.get(rg))
        for rg in writes:
            add_dep(self.last_w.get(rg))
            for ev in self.readers.get(rg, ()):
                add_dep(ev)
        for sm, c in needed.items():
            ev = self._ev_by_sem.get((sm, c))
            if ev is not None:
                snap = ev[4]
                for s2, c2 in snap.items():
                    if self.guar[stream].get(s2, 0) < c2:
                        self.guar[stream][s2] = c2
            if self.guar[stream].get(sm, 0) < c:
                self.guar[stream][sm] = c
        self.counts[sem] += inc
        cnt = self.counts[sem]
        snap = dict(self.guar[stream])
        snap[sem] = cnt
        ev = (stream, sem, cnt, is_dma, snap)
        self._ev_by_sem[(sem, cnt)] = ev
        for rg in writes:
            self.last_w[rg] = ev
            self.readers[rg] = []
        for rg in reads:
            self.readers.setdefault(rg, []).append(ev)
        self.ops[stream].append((fn, dict(needed), sem, inc))

    _ev_by_sem = {}

    def dma(self, out, in_, eng="sync"):
        sem = "gdma" if eng == "gpsimd" else eng
        self.emit_op(eng, lambda e, out=out, in_=in_: e.dma_start(out=out, in_=in_),
                     _regs(in_), _regs(out), is_dma=True, sem=sem)

    def idma_gather(self, out, in_, idxcol):
        self.emit_op("gpsimd", lambda e, out=out, in_=in_, idx=idxcol:
                     e.indirect_dma_start(out, None, in_, bass.IndirectOffsetOnAxis(idx, 0)),
                     _regs(in_, idxcol), _regs(out), is_dma=True, sem="gdma")

    def tt(self, out, a, b, op, eng="vector"):
        self.emit_op(eng, lambda e, out=out, a=a, b=b, op=op: e.tensor_tensor(out, a, b, op),
                     _regs(a, b), _regs(out))

    def ts(self, out, a, s1, s2, op0, op1=None, accum=None, eng="vector"):
        if op1 is None:
            self.emit_op(eng, lambda e, out=out, a=a, s1=s1, op0=op0:
                         e.tensor_scalar(out, a, s1, None, op0),
                         _regs(a, s1), _regs(out))
        else:
            self.emit_op(eng, lambda e, out=out, a=a, s1=s1, s2=s2, op0=op0, op1=op1, ac=accum:
                         e.tensor_scalar(out, a, s1, s2, op0, op1, accum_out=ac),
                         _regs(a, s1, s2), _regs(out, accum))

    def stt(self, out, a, s, b, op0, op1, eng="vector"):
        self.emit_op(eng, lambda e, out=out, a=a, s=s, b=b, op0=op0, op1=op1:
                     e.scalar_tensor_tensor(out, a, s, b, op0, op1),
                     _regs(a, s, b), _regs(out))

    def act(self, out, in_, fn, bias=None, scale=None, accum=None):
        kw = {}
        if bias is not None:
            kw["bias"] = bias
        if scale is not None:
            kw["scale"] = scale
        if accum is not None:
            kw["accum_out"] = accum
        self.emit_op("scalar", lambda e, out=out, in_=in_, fn=fn, kw=kw:
                     e.activation(out, in_, fn, **kw),
                     _regs(in_, bias), _regs(out, accum))

    def copy(self, out, in_, eng="vector"):
        if eng == "scalar":
            self.emit_op(eng, lambda e, out=out, in_=in_:
                         e.activation(out, in_, AF.Copy),
                         _regs(in_), _regs(out))
        else:
            self.emit_op(eng, lambda e, out=out, in_=in_: e.tensor_copy(out, in_),
                         _regs(in_), _regs(out))

    def scan(self, out, d0, d1, init, eng="vector"):
        self.emit_op(eng, lambda e, out=out, d0=d0, d1=d1, init=init:
                     e.tensor_tensor_scan(out, d0, d1, init, OP.mult, OP.add),
                     _regs(d0, d1, init), _regs(out))

    def red(self, out, in_, op=OP.add, eng="vector"):
        self.emit_op(eng, lambda e, out=out, in_=in_, op=op:
                     e.tensor_reduce(out, in_, AXX, op),
                     _regs(in_), _regs(out))

    def mm(self, out, lhsT, rhs, start, stop):
        n = rhs.shape[-1]

        def fn(e, out=out, l=lhsT, r=rhs, st=start, sp=stop, n=n):
            if n <= 512:
                return e.matmul(out, l, r, start=st, stop=sp)
            ret = None
            for o in range(0, n, 512):
                w = min(512, n - o)
                ret = e.matmul(out[:, o:o + w], l, r[:, o:o + w], start=st, stop=sp)
            return ret
        reads = _regs(lhsT, rhs) + ([] if start else _regs(out))
        self.emit_op("tensor", fn, reads, _regs(out))

    def tp(self, out, in_, ident):
        self.emit_op("tensor", lambda e, out=out, in_=in_, id_=ident:
                     e.transpose(out, in_, id_),
                     _regs(in_, ident), _regs(out))

    def memset(self, ap, v, eng="gpsimd"):
        self.emit_op(eng, lambda e, ap=ap, v=v: e.memset(ap, v), [], _regs(ap))

    def iota(self, out, pattern, base, cm):
        self.emit_op("gpsimd", lambda e, out=out, p=pattern, b=base, cm=cm:
                     e.iota(out, pattern=p, base=b, channel_multiplier=cm,
                            allow_small_or_imprecise_dtypes=True),
                     [], _regs(out))

    def cc(self, kind, op, ins, outs):
        self.emit_op("gpsimd", lambda e, kind=kind, op=op, ins=ins, outs=outs:
                     e.collective_compute(kind, op, replica_groups=[list(range(NCORE))],
                                          ins=ins, outs=outs),
                     [t.tensor.name for t in ins], [t.tensor.name for t in outs],
                     is_dma=True, sem="ccol", inc=1)


def build_nc(stage=9, taps=()):
    tapset = set(taps)
    nc = bass.Bass("TRN2", target_bir_lowering=False)

    def din(name, shape, dt=F32):
        return nc.dram_tensor(name, list(shape), dt, kind="ExternalInput")

    x_my = din("x_my", (T, C))
    x_full = din("x_full", (TT_ALL, C))
    wtk = din("wtk", (128, 8, C)); wtv = din("wtv", (128, 8, C)); wtr = din("wtr", (128, 8, C))
    wcr = din("wcr", (128, 8, C))
    wck = din("wck", (32, 128, 8, 128)); wcv = din("wcv", (8, 128, 32, 128))
    w1 = din("w1", (32, 128, 8, 128)); w2 = din("w2", (8, 128, 32, 128))
    b1c_d = din("b1c", (128, 32))
    b2fc_d = din("b2fc", (128, 8))
    tdc_d = din("tdc", (128, 8)); tfc_d = din("tfc", (128, 8))
    mkc_d = din("mkc", (128, 8)); mvc_d = din("mvc", (128, 8)); mrc_d = din("mrc", (128, 8))
    cmkc_d = din("cmkc", (128, 8)); cmrc_d = din("cmrc", (128, 8))
    g1c_d = din("g1c", (128, 8)); b1lc_d = din("b1lc", (128, 8))
    g2c_d = din("g2c", (128, 8)); b2lc_d = din("b2lc", (128, 8))
    gfc_d = din("gfc", (128, 8)); bfc_d = din("bfc", (128, 8))
    centT_d = din("centT", (128, 8, 8))
    ident_d = din("identf", (128, 128))
    identb_d = din("identb", (128, 128), BF16)
    trie_d = din("trie", (128, 128))
    ones_d = din("onesf", (128, 128))
    sel16_d = din("sel16", (128, 16))
    ohc_d = din("ohc", (128, 8))

    out_d = nc.dram_tensor("out", [T, C], F32, kind="ExternalOutput")
    tap_d = {}

    def mktap(name, shape, dt=F32):
        if name in tapset:
            tap_d[name] = nc.dram_tensor("tap_" + name, list(shape), dt, kind="ExternalOutput")

    for nm, shp, dt in [("xn", (T, C), F32), ("scout", (64, T), F32),
                        ("masks", (128, 1024), F32), ("slot", (128, 128), F32),
                        ("idx", (128, 16), I32), ("myslot", (128, 16), I32),
                        ("r", (T, C), F32), ("x1", (T, C), F32), ("kdbg", (C, T), F32),
                        ("vdbg", (C, T), F32), ("ydbg", (C, T), F32), ("ppdbg", (128, 513), F32), ("aadbg", (128, 513), F32), ("dldbg", (128, 512), F32),
                        ("s", (T, C), F32), ("gin", (T, C), F32)]:
        mktap(nm, shp, dt)

    xn_d = nc.dram_tensor("xn_d", [T, C], F32)
    x1_d = nc.dram_tensor("x1_d", [T, C], F32)
    s_dd = nc.dram_tensor("s_dd", [T, C], F32)
    r_d = nc.dram_tensor("r_d", [T, C], F32)
    k_dbg_d = nc.dram_tensor("k_dbg_d", [C, T], F32) if "kdbg" in tapset else None
    v_dbg_d = nc.dram_tensor("v_dbg_d", [C, T], F32) if "vdbg" in tapset else None
    y_dbg_d = nc.dram_tensor("y_dbg_d", [C, T], F32) if "ydbg" in tapset else None
    xng_d = nc.dram_tensor("xng_d", [T, C], F32)
    rowcum_d = nc.dram_tensor("rowcum_d", [128, 128], F32)
    sc_in = nc.dram_tensor("sc_in", [8, T], F32)
    sc_out = nc.dram_tensor("sc_out", [8 * NCORE, T], F32)
    g_in = nc.dram_tensor("g_in", [T, C], BF16)
    g_out = nc.dram_tensor("g_out", [TT_ALL, C], BF16)

    st = ExitStack()
    with st:
        blk = st.enter_context(nc.Block())
        sems = {s: st.enter_context(nc.semaphore("sem_" + s))
                for s in ["sync", "vector", "scalar", "gpsimd", "tensor", "gdma", "ccol"]}
        P = Prog(nc)
        P._ev_by_sem = {}

        def sb(stk, name, shape, dt=F32):
            return stk.enter_context(nc.sbuf_tensor("sb_" + name, list(shape), dt))

        def psum(stk, name, shape, dt=F32):
            return stk.enter_context(nc.psum_tensor("ps_" + name, list(shape), dt))

        tdc = sb(st, "tdc", (128, 8)); tfc = sb(st, "tfc", (128, 8))
        ewc = sb(st, "ewc", (128, 8))
        mkc = sb(st, "mkc", (128, 8)); mvc = sb(st, "mvc", (128, 8)); mrc = sb(st, "mrc", (128, 8))
        cmkc = sb(st, "cmkc", (128, 8)); cmrc = sb(st, "cmrc", (128, 8))
        g1c = sb(st, "g1c", (128, 8)); b1lc = sb(st, "b1lc", (128, 8))
        g2c = sb(st, "g2c", (128, 8)); b2lc = sb(st, "b2lc", (128, 8))
        gfc = sb(st, "gfc", (128, 8)); bfc = sb(st, "bfc", (128, 8))
        b1c = sb(st, "b1c", (128, 32)); b2fc = sb(st, "b2fc", (128, 8))
        identf = sb(st, "identf", (128, 128)); identb = sb(st, "identb", (128, 128), BF16)
        trie = sb(st, "trie", (128, 128)); onesf = sb(st, "onesf", (128, 128))
        sel16 = sb(st, "sel16", (128, 16)); ohc = sb(st, "ohc", (128, 8))
        iot16 = sb(st, "iot16", (128, 16))
        idxcols = sb(st, "idxcols", (128, 16), I32)
        myslots = sb(st, "myslots", (128, 16), I32)
        s1c = sb(st, "s1c", (128, 1)); s2c = sb(st, "s2c", (128, 1))
        muc = sb(st, "muc", (128, 1)); varc = sb(st, "varc", (128, 1)); rsd = sb(st, "rsd", (128, 1))
        tva = sb(st, "tva", (128, 1)); tvb = sb(st, "tvb", (128, 1))
        xt = sb(st, "xt", (128, C)); sq_s = sb(st, "sq_s", (128, C))
        nxt = sb(st, "nxt", (128, C), BF16)
        c_eps = sb(st, "c_eps", (128, 1)); c_one = sb(st, "c_one", (128, 1))
        nx1T = sb(st, "nx1T", (128, 8, 2049), BF16)
        trunk = ExitStack()
        nxnT = sb(trunk, "nxnT", (128, 8, 2049), BF16)

        for tile, d in [(tdc, tdc_d), (tfc, tfc_d), (mkc, mkc_d), (mvc, mvc_d), (mrc, mrc_d),
                        (cmkc, cmkc_d), (cmrc, cmrc_d), (g1c, g1c_d), (b1lc, b1lc_d),
                        (g2c, g2c_d), (b2lc, b2lc_d), (gfc, gfc_d), (bfc, bfc_d),
                        (b1c, b1c_d), (b2fc, b2fc_d), (identf, ident_d), (identb, identb_d),
                        (trie, trie_d), (onesf, ones_d), (sel16, sel16_d), (ohc, ohc_d)]:
            P.dma(tile[:, :], d[:, :])
        P.iota(iot16[:, :], [[128, 16]], 0, 1)
        P.memset(c_eps[:, 0:1], float(EPS))
        P.memset(c_one[:, 0:1], 1.0)
        P.act(ewc[:, :], tdc[:, :], AF.Exp)

        def ln_stats(src, w):
            P.act(sq_s[:, 0:w], src, AF.Copy, accum=s1c[:, 0:1])
            P.tt(sq_s[:, 0:w], src, src, OP.mult)
            P.act(sq_s[:, 0:w], sq_s[:, 0:w], AF.Copy, accum=s2c[:, 0:1])
            P.ts(muc[:, 0:1], s1c[:, 0:1], 1.0 / w, None, OP.mult)
            P.tt(tva[:, 0:1], muc[:, 0:1], muc[:, 0:1], OP.mult)
            P.ts(varc[:, 0:1], s2c[:, 0:1], 1.0 / w, tva[:, 0:1], OP.mult, OP.subtract)
            P.act(tvb[:, 0:1], varc[:, 0:1], AF.Ln, bias=c_eps[:, 0:1])
            P.act(rsd[:, 0:1], tvb[:, 0:1], AF.Exp, scale=-0.5)

        with ExitStack() as p1:
            centTb = sb(p1, "centTb", (128, 8, 8), BF16)
            scev = sb(p1, "scev", (8, T))
            hfT_t = sb(p1, "hfT_t", (128, T), BF16)
            tppsb = psum(p1, "tppsb", (128, 128), BF16)
            scps = psum(p1, "scps", (8, T))
            P.dma(centTb[:, :, :], centT_d[:, :, :], eng="gpsimd")
            for cc in range(8):
                P.memset(nxnT[:, cc, 0:1], 0.0)
            for i in range(16):
                P.dma(xt[:, :], x_my[i * 128:(i + 1) * 128, :])
                ln_stats(xt[:, :], C)
                P.ts(xt[:, :], xt[:, :], muc[:, 0:1], rsd[:, 0:1], OP.subtract, OP.mult)
                P.dma(xn_d[i * 128:(i + 1) * 128, :], xt[:, :], eng="scalar")
                ln_stats(xt[:, :], C)
                P.ts(nxt[:, :], xt[:, :], muc[:, 0:1], rsd[:, 0:1], OP.subtract, OP.mult)
                for cc in range(8):
                    P.tp(tppsb[:, :], nxt[:, cc * 128:(cc + 1) * 128], identb[:, :])
                    P.copy(nxnT[:, cc, 1 + i * 128:1 + (i + 1) * 128], tppsb[:, :])
            for cc in range(8):
                P.ts(hfT_t[:, :], nxnT[:, cc, 1:2049], gfc[:, cc:cc + 1], bfc[:, cc:cc + 1],
                     OP.mult, OP.add)
                for half in range(2):
                    P.mm(scps[0:8, half * 1024:(half + 1) * 1024], centTb[:, cc, :],
                         hfT_t[:, half * 1024:(half + 1) * 1024],
                         start=(cc == 0), stop=(cc == 7))
            P.copy(scev[0:8, :], scps[0:8, :])
            P.dma(sc_in[:, :], scev[0:8, :])
            P.cc("AllGather", OP.bypass, [sc_in.ap().opt()], [sc_out.ap().opt()])
            if "xn" in tapset:
                P.dma(tap_d["xn"][:, :], xn_d[:, :])
            if "scout" in tapset:
                P.dma(tap_d["scout"][:, :], sc_out[:, :])
        P.barrier()

        if stage >= 2:
            with ExitStack() as pr:
                s8 = sb(pr, "s8", (128, 1024))
                takenb = sb(pr, "takenb", (128, 128))
                masksb = sb(pr, "masksb", (128, 1024))
                slotb = sb(pr, "slotb", (128, 128))
                myrc = sb(pr, "myrc", (128, 128))
                myoc = sb(pr, "myoc", (128, 1))
                r_t1 = sb(pr, "r_t1", (128, 128)); r_t2 = sb(pr, "r_t2", (128, 128))
                r_t3 = sb(pr, "r_t3", (128, 128))
                rc1 = sb(pr, "rc1", (128, 1)); rc2 = sb(pr, "rc2", (128, 1))
                rc3 = sb(pr, "rc3", (128, 1)); rc4 = sb(pr, "rc4", (128, 1))
                rc5 = sb(pr, "rc5", (128, 1))
                lob = sb(pr, "lob", (128, 1)); wda = sb(pr, "wda", (128, 1))
                wdb = sb(pr, "wdb", (128, 1)); hib = sb(pr, "hib", (128, 1))
                lo0b = sb(pr, "lo0b", (128, 1)); hi0b = sb(pr, "hi0b", (128, 1))
                obb = sb(pr, "obb", (128, 128)); orow = sb(pr, "orow", (128, 128))
                pjc = sb(pr, "pjc", (128, 1)); ljc = sb(pr, "ljc", (128, 1))
                posc = sb(pr, "posc", (128, 1)); pji = sb(pr, "pji", (128, 1), I32)
                RG = sb(pr, "RG", (128, 128))
                cntps = psum(pr, "cntps", (128, 1))
                tppsf = psum(pr, "tppsf", (128, 128))
                selps = psum(pr, "selps", (16, 128))

                for e in range(8):
                    ap_in = bass.AP(sc_out, e * T, [[8 * T, 8], [128, 16], [1, 128]])
                    P.dma(s8[:, e * 128:(e + 1) * 128], ap_in)

                def part_reduce(vec_col, op, neutral, dst_col):
                    P.memset(r_t2[:, :], neutral)
                    P.copy(r_t2[:, 0:1], vec_col)
                    P.tp(tppsf[:, :], r_t2[:, :], identf[:, :])
                    P.copy(r_t3[:, :], tppsf[:, :])
                    P.red(rc2[0:1, 0:1], r_t3[0:1, 0:128], op)
                    P.mm(cntps[:, 0:1], onesf[0:1, :], rc2[0:1, 0:1], start=True, stop=True)
                    P.copy(dst_col, cntps[:, 0:1])

                P.red(rc1[:, 0:1], s8[:, :], OP.max)
                part_reduce(rc1[:, 0:1], OP.max, NEGBIG, hi0b[:, 0:1])
                P.ts(hi0b[:, 0:1], hi0b[:, 0:1], 1.0, None, OP.add)
                P.red(rc1[:, 0:1], s8[:, :], OP.min)
                part_reduce(rc1[:, 0:1], OP.min, POSBIG, lo0b[:, 0:1])
                P.ts(lo0b[:, 0:1], lo0b[:, 0:1], -1.0, None, OP.add)

                P.memset(takenb[:, :], 0.0)
                for e in range(8):
                    if e == 7:
                        P.ts(masksb[:, 7 * 128:8 * 128], takenb[:, :], -1.0, 1.0,
                             OP.mult, OP.add)
                        break
                    P.stt(r_t1[:, :], takenb[:, :], NEGBIG, s8[:, e * 128:(e + 1) * 128],
                          OP.mult, OP.add)
                    P.copy(lob[:, 0:1], lo0b[:, 0:1])
                    P.tt(wda[:, 0:1], hi0b[:, 0:1], lo0b[:, 0:1], OP.subtract)
                    wcur, wnext = wda, wdb
                    for it in range(BIS_ITERS):
                        P.ts(wnext[:, 0:1], wcur[:, 0:1], 0.5, None, OP.mult)
                        P.tt(rc3[:, 0:1], lob[:, 0:1], wnext[:, 0:1], OP.add)
                        P.ts(r_t2[:, :], r_t1[:, :], rc3[:, 0:1], None, OP.is_gt, OP.add,
                             accum=rc4[:, 0:1])
                        P.mm(cntps[:, 0:1], onesf[:, :], rc4[:, 0:1], start=True, stop=True)
                        P.ts(rc5[:, 0:1], cntps[:, 0:1], float(T), None, OP.is_ge)
                        P.tt(rc4[:, 0:1], rc5[:, 0:1], wnext[:, 0:1], OP.mult)
                        P.tt(lob[:, 0:1], lob[:, 0:1], rc4[:, 0:1], OP.add)
                        wcur, wnext = wnext, wcur
                    P.tt(hib[:, 0:1], lob[:, 0:1], wcur[:, 0:1], OP.add)
                    P.ts(r_t2[:, :], r_t1[:, :], hib[:, 0:1], None, OP.is_gt, OP.add,
                         accum=rc4[:, 0:1])
                    P.mm(cntps[:, 0:1], onesf[:, :], rc4[:, 0:1], start=True, stop=True)
                    P.ts(rc5[:, 0:1], cntps[:, 0:1], -1.0, float(T), OP.mult, OP.add)
                    P.ts(r_t3[:, :], r_t1[:, :], lob[:, 0:1], None, OP.is_gt)
                    P.tt(r_t3[:, :], r_t3[:, :], r_t2[:, :], OP.subtract)
                    P.scan(r_t1[:, :], onesf[:, :], r_t3[:, :], 0.0)
                    P.mm(cntps[:, 0:1], trie[:, :], r_t1[:, 127:128], start=True, stop=True)
                    P.copy(rc1[:, 0:1], cntps[:, 0:1])
                    P.ts(r_t1[:, :], r_t1[:, :], rc1[:, 0:1], None, OP.add)
                    P.ts(r_t1[:, :], r_t1[:, :], rc5[:, 0:1], None, OP.is_le)
                    P.tt(r_t3[:, :], r_t3[:, :], r_t1[:, :], OP.mult)
                    P.tt(masksb[:, e * 128:(e + 1) * 128], r_t2[:, :], r_t3[:, :], OP.add)
                    P.tt(takenb[:, :], takenb[:, :], masksb[:, e * 128:(e + 1) * 128], OP.add)

                P.memset(slotb[:, :], 0.0)
                P.memset(myrc[:, :], 0.0)
                P.memset(myoc[:, 0:1], 0.0)
                for e in range(8):
                    me = masksb[:, e * 128:(e + 1) * 128]
                    P.scan(r_t2[:, :], onesf[:, :], me, 0.0)
                    P.mm(cntps[:, 0:1], trie[:, :], r_t2[:, 127:128], start=True, stop=True)
                    P.copy(rc1[:, 0:1], cntps[:, 0:1])
                    P.ts(r_t3[:, :], r_t2[:, :], rc1[:, 0:1], None, OP.add)
                    P.ts(r_t3[:, :], r_t3[:, :], float(e * T - 1), None, OP.add)
                    P.tt(r_t3[:, :], r_t3[:, :], me, OP.mult)
                    P.tt(slotb[:, :], slotb[:, :], r_t3[:, :], OP.add)
                    P.ts(r_t3[:, :], r_t2[:, :], ohc[:, e:e + 1], None, OP.mult)
                    P.tt(myrc[:, :], myrc[:, :], r_t3[:, :], OP.add)
                    P.ts(rc2[:, 0:1], rc1[:, 0:1], ohc[:, e:e + 1], None, OP.mult)
                    P.tt(myoc[:, 0:1], myoc[:, 0:1], rc2[:, 0:1], OP.add)

                P.mm(selps[0:16, :], sel16[:, :], slotb[:, :], start=True, stop=True)
                P.copy(r_t2[0:16, :], selps[0:16, :])
                P.tp(tppsf[:, :], r_t2[:, :], identf[:, :])
                P.copy(r_t3[:, :], tppsf[:, :])
                P.copy(myslots[:, :], r_t3[:, 0:16])

                P.dma(rowcum_d[:, :], myrc[:, :])
                P.memset(r_t2[:, :], 0.0)
                P.copy(r_t2[:, 0:1], myoc[:, 0:1])
                P.tp(tppsf[:, :], r_t2[:, :], identf[:, :])
                P.copy(orow[:, :], tppsf[:, :])
                P.mm(tppsf[:, :], onesf[0:1, :], orow[0:1, :], start=True, stop=True)
                P.copy(obb[:, :], tppsf[:, :])
                for i in range(16):
                    jcol = iot16[:, i:i + 1]
                    P.ts(r_t2[:, :], obb[:, :], jcol, None, OP.is_le, OP.add,
                         accum=pjc[:, 0:1])
                    P.ts(pjc[:, 0:1], pjc[:, 0:1], -1.0, None, OP.add)
                    P.tt(r_t3[:, :], obb[:, :], r_t2[:, :], OP.mult)
                    P.red(rc1[:, 0:1], r_t3[:, :], OP.max)
                    P.tt(ljc[:, 0:1], jcol, rc1[:, 0:1], OP.subtract)
                    P.copy(pji[:, 0:1], pjc[:, 0:1])
                    P.idma_gather(RG[:, :], rowcum_d[0:128, 0:128], pji[0:128, 0:1])
                    P.ts(r_t2[:, :], RG[:, :], ljc[:, 0:1], None, OP.is_le, OP.add,
                         accum=posc[:, 0:1])
                    P.ts(rc1[:, 0:1], pjc[:, 0:1], 128.0, posc[:, 0:1], OP.mult, OP.add)
                    P.copy(idxcols[:, i:i + 1], rc1[:, 0:1])

                if "masks" in tapset:
                    P.dma(tap_d["masks"][:, :], masksb[:, :])
                if "slot" in tapset:
                    P.dma(tap_d["slot"][:, :], slotb[:, :])
                if "idx" in tapset:
                    P.dma(tap_d["idx"][:, :], idxcols[:, :])
                if "myslot" in tapset:
                    P.dma(tap_d["myslot"][:, :], myslots[:, :])
            P.barrier()

        if stage >= 3:
            with ExitStack() as p4:
                wtk_s = sb(p4, "wtk_s", (128, 8, C), BF16)
                wtv_s = sb(p4, "wtv_s", (128, 8, C), BF16)
                mixd = sb(p4, "mixd", (128, 512), BF16)
                xkTg = sb(p4, "xkTg", (128, 8, 512), BF16)
                xvTg = sb(p4, "xvTg", (128, 8, 512), BF16)
                kbufl = [sb(p4, "kbuf_a", (128, 512)), sb(p4, "kbuf_b", (128, 512))]
                vbufl = [sb(p4, "vbuf_a", (128, 512)), sb(p4, "vbuf_b", (128, 512))]
                yTtg = sb(p4, "yTtg", (128, 8, 512))
                ones512 = sb(p4, "ones512", (128, 512))
                ppt0 = sb(p4, "ppt0", (128, 513))
                aat0 = sb(p4, "aat0", (128, 513))
                ppcar = sb(p4, "ppcar", (128, 8)); aacar = sb(p4, "aacar", (128, 8))
                wpc = sb(p4, "wpc", (128, 1))
                sc1 = sb(p4, "sc1", (128, 512)); sc2 = sb(p4, "sc2", (128, 512))
                sc3 = sb(p4, "sc3", (128, 512)); sc4 = sb(p4, "sc4", (128, 512))
                sc5 = sb(p4, "sc5", (128, 512)); sc6 = sb(p4, "sc6", (128, 512))
                yt_f = sb(p4, "yt_f", (128, C)); rt_f = sb(p4, "rt_f", (128, C))
                x1t = sb(p4, "x1t", (128, C))
                mmps_a = psum(p4, "mmps_a", (128, 512))
                mmps_b = psum(p4, "mmps_b", (128, 512))
                mmps_w = psum(p4, "mmps_w", (128, 1024))
                tppsf4 = psum(p4, "tppsf4", (128, 128))
                tppsb2 = psum(p4, "tppsb2", (128, 128), BF16)

                for wsb, wd in [(wtk_s, wtk), (wtv_s, wtv)]:
                    P.dma(wsb[:, :, :], wd[:, :, :], eng="gpsimd")
                P.memset(ones512[:, :], 1.0)
                P.memset(ppcar[:, :], 0.0)
                P.memset(aacar[:, :], 0.0)

                for cc in range(8):
                    P.ts(nxnT[:, cc, 1:2049], nxnT[:, cc, 1:2049], g1c[:, cc:cc + 1],
                         b1lc[:, cc:cc + 1], OP.mult, OP.add)
                hT = nxnT

                with ExitStack() as p3:
                    wtr_s = sb(p3, "wtr_s", (128, 8, C), BF16)
                    xr_t = sb(p3, "xr_t", (128, 8, 128), BF16)
                    P.dma(wtr_s[:, :, :], wtr[:, :, :], eng="gpsimd")
                    for ti in range(16):
                        for cc in range(8):
                            hp = hT[:, cc, ti * 128:(ti + 1) * 128]
                            hh = hT[:, cc, 1 + ti * 128:1 + (ti + 1) * 128]
                            P.tt(xr_t[:, cc, :], hh, hp, OP.subtract)
                            P.stt(xr_t[:, cc, :], xr_t[:, cc, :], mrc[:, cc:cc + 1], hp,
                                  OP.mult, OP.add)
                        for dh in range(8):
                            P.mm(mmps_w[:, :], xr_t[:, dh, :], wtr_s[:, dh, :],
                                 start=(dh == 0), stop=(dh == 7))
                        P.act(rt_f[:, :], mmps_w[:, :], AF.Sigmoid)
                        P.dma(r_d[ti * 128:(ti + 1) * 128, :], rt_f[:, :])
                if "r" in tapset:
                    P.dma(tap_d["r"][:, :], r_d[:, :])

                for cc in range(8):
                    P.memset(nx1T[:, cc, 0:1], 0.0)
                for tg in range(4):
                    t0 = tg * 512
                    for cc in range(8):
                        hp = hT[:, cc, t0:t0 + 512]
                        hh = hT[:, cc, 1 + t0:1 + t0 + 512]
                        P.tt(mixd[:, :], hh, hp, OP.subtract)
                        P.stt(xkTg[:, cc, :], mixd[:, :], mkc[:, cc:cc + 1], hp,
                              OP.mult, OP.add)
                        P.stt(xvTg[:, cc, :], mixd[:, :], mvc[:, cc:cc + 1], hp,
                              OP.mult, OP.add)
                    for cc in range(8):
                        kbuf = kbufl[cc % 2]
                        vbuf = vbufl[cc % 2]
                        for dh in range(8):
                            P.mm(mmps_a[:, :], wtk_s[:, dh, cc * 128:(cc + 1) * 128],
                                 xkTg[:, dh, :], start=(dh == 0), stop=(dh == 7))
                        P.copy(kbuf[:, :], mmps_a[:, :], eng="scalar")
                        for dh in range(8):
                            P.mm(mmps_b[:, :], wtv_s[:, dh, cc * 128:(cc + 1) * 128],
                                 xvTg[:, dh, :], start=(dh == 0), stop=(dh == 7))
                        P.copy(vbuf[:, :], mmps_b[:, :], eng="scalar")
                        if k_dbg_d is not None:
                            P.dma(k_dbg_d[cc * 128:(cc + 1) * 128, t0:t0 + 512], kbuf[:, :])
                        if v_dbg_d is not None:
                            P.dma(v_dbg_d[cc * 128:(cc + 1) * 128, t0:t0 + 512], vbuf[:, :])
                        kk_ = kbuf[:, :]; vv_ = vbuf[:, :]
                        pc = ppcar[:, cc:cc + 1]; ac = aacar[:, cc:cc + 1]
                        pt = ppt0; at = aat0
                        P.ts(sc6[:, :], ones512[:, :], ewc[:, cc:cc + 1], None, OP.mult)
                        P.ts(wpc[:, 0:1], tdc[:, cc:cc + 1], pc, None, OP.subtract)
                        P.ts(sc1[:, :], kk_, wpc[:, 0:1], None, OP.add)
                        P.act(sc2[:, :], sc1[:, :], AF.Exp)
                        P.scan(sc3[:, :], sc6[:, :], sc2[:, :], 0.0 if tg == 0 else 1.0)
                        if cc == 0 and tg == 0 and "dldbg" in tapset:
                            P.dma(tap_d["dldbg"][:, :], sc3[:, :])
                        P.act(sc4[:, :], sc3[:, :], AF.Ln)
                        if tg == 0:
                            P.memset(pt[:, 0:1], PPINIT)
                        else:
                            P.copy(pt[:, 0:1], pc)
                        P.ts(pt[:, 1:513], sc4[:, :], pc, None, OP.add)
                        P.copy(ppcar[:, cc:cc + 1], pt[:, 512:513])
                        P.tt(sc1[:, :], pt[:, 0:512], kk_, OP.subtract)
                        P.ts(sc2[:, :], sc1[:, :], 0.0, None, OP.min)
                        P.act(sc3[:, :], sc2[:, :], AF.Exp)
                        P.ts(sc2[:, :], sc1[:, :], 0.0, None, OP.max)
                        P.act(sc4[:, :], sc2[:, :], AF.Exp, scale=-1.0)
                        P.tt(sc5[:, :], sc4[:, :], vv_, OP.mult, eng="gpsimd")
                        P.copy(at[:, 0:1], ac)
                        P.scan(at[:, 1:513], sc3[:, :], sc5[:, :], ac)
                        P.copy(aacar[:, cc:cc + 1], at[:, 512:513])
                        P.ts(sc1[:, :], sc1[:, :], tfc[:, cc:cc + 1], None, OP.subtract)
                        P.ts(sc2[:, :], sc1[:, :], 0.0, None, OP.min)
                        P.act(sc3[:, :], sc2[:, :], AF.Exp)
                        P.ts(sc2[:, :], sc1[:, :], 0.0, None, OP.max)
                        P.act(sc4[:, :], sc2[:, :], AF.Exp, scale=-1.0)
                        P.tt(sc5[:, :], sc3[:, :], at[:, 0:512], OP.mult)
                        P.tt(sc2[:, :], sc4[:, :], vv_, OP.mult, eng="gpsimd")
                        P.tt(sc5[:, :], sc5[:, :], sc2[:, :], OP.add)
                        P.tt(sc2[:, :], sc3[:, :], sc4[:, :], OP.mult, eng="gpsimd")
                        P.act(sc3[:, :], sc2[:, :], AF.Ln, bias=c_one[:, 0:1])
                        P.act(sc2[:, :], sc3[:, :], AF.Exp, scale=-1.0)
                        P.tt(yTtg[:, cc, :], sc5[:, :], sc2[:, :], OP.mult)
                        if y_dbg_d is not None:
                            P.dma(y_dbg_d[cc * 128:(cc + 1) * 128, t0:t0 + 512],
                                  yTtg[:, cc, :])
                        if cc == 0 and tg == 0 and "ppdbg" in tapset:
                            P.dma(tap_d["ppdbg"][:, :], pt[:, :])
                            P.dma(tap_d["aadbg"][:, :], at[:, :])
                    for ti in range(4):
                        tt_ = tg * 4 + ti
                        for cc in range(8):
                            P.tp(tppsf4[:, :], yTtg[:, cc, ti * 128:(ti + 1) * 128],
                                 identf[:, :])
                            P.copy(yt_f[:, cc * 128:(cc + 1) * 128], tppsf4[:, :])
                        P.dma(rt_f[:, :], r_d[tt_ * 128:(tt_ + 1) * 128, :], eng="scalar")
                        P.dma(xt[:, :], xn_d[tt_ * 128:(tt_ + 1) * 128, :], eng="scalar")
                        P.tt(x1t[:, :], rt_f[:, :], yt_f[:, :], OP.mult)
                        P.tt(x1t[:, :], x1t[:, :], xt[:, :], OP.add)
                        P.dma(x1_d[tt_ * 128:(tt_ + 1) * 128, :], x1t[:, :], eng="scalar")
                        ln_stats(x1t[:, :], C)
                        P.ts(nxt[:, :], x1t[:, :], muc[:, 0:1], rsd[:, 0:1],
                             OP.subtract, OP.mult)
                        for cc in range(8):
                            P.tp(tppsb2[:, :], nxt[:, cc * 128:(cc + 1) * 128], identb[:, :])
                            P.copy(nx1T[:, cc, 1 + tt_ * 128:1 + (tt_ + 1) * 128],
                                   tppsb2[:, :])
                if "x1" in tapset:
                    P.dma(tap_d["x1"][:, :], x1_d[:, :])
                if k_dbg_d is not None:
                    P.dma(tap_d["kdbg"][:, :], k_dbg_d[:, :])
                if v_dbg_d is not None:
                    P.dma(tap_d["vdbg"][:, :], v_dbg_d[:, :])
                if y_dbg_d is not None:
                    P.dma(tap_d["ydbg"][:, :], y_dbg_d[:, :])

            P.barrier()
        trunk.close()

        if stage >= 4:
            with ExitStack() as p5:
                wcr_s = sb(p5, "wcr_s", (128, 8, C), BF16)
                ckTg = sb(p5, "ckTg", (128, 8, 1024), BF16)
                crTg = sb(p5, "crTg", (128, 8, 1024), BF16)
                kkT = sb(p5, "kkT", (128, 32, 1024), BF16)
                cvstage = sb(p5, "cvstage", (128, 8, 1024))
                wstr = [sb(p5, "wstr_a", (128, 8, 128), BF16),
                        sb(p5, "wstr_b", (128, 8, 128), BF16)]
                wstr2 = [sb(p5, "wstr2_a", (128, 16, 128), BF16),
                         sb(p5, "wstr2_b", (128, 16, 128), BF16)]
                dscr = nxt
                ev2 = sb(p5, "ev2", (128, 1024))
                ev1l = [sq_s, ev2]
                ev1 = sq_s
                pwl = [psum(p5, "pw_a", (128, 1024)), psum(p5, "pw_b", (128, 1024))]
                pw2 = psum(p5, "pw2", (128, 1024))
                tppsf5l = [psum(p5, "tppsf5a", (128, 128)), psum(p5, "tppsf5b", (128, 128))]

                P.dma(wcr_s[:, :, :], wcr[:, :, :], eng="gpsimd")
                for cc in range(8):
                    P.ts(nx1T[:, cc, 1:2049], nx1T[:, cc, 1:2049], g2c[:, cc:cc + 1],
                         b2lc[:, cc:cc + 1], OP.mult, OP.add)
                for tg in range(2):
                    t0 = tg * 1024
                    for cc in range(8):
                        hp = nx1T[:, cc, t0:t0 + 1024]
                        hh = nx1T[:, cc, 1 + t0:1 + t0 + 1024]
                        P.tt(dscr[:, :], hh, hp, OP.subtract)
                        P.stt(ckTg[:, cc, :], dscr[:, :], cmkc[:, cc:cc + 1], hp,
                              OP.mult, OP.add)
                        P.stt(crTg[:, cc, :], dscr[:, :], cmrc[:, cc:cc + 1], hp,
                              OP.mult, OP.add)
                    for f in range(32):
                        wb = wstr[f % 2]
                        pw = pwl[f % 2]
                        P.dma(wb[:, :, :], wck[f], eng="gpsimd")
                        for dh in range(8):
                            P.mm(pw[:, :], wb[:, dh, :], ckTg[:, dh, :],
                                 start=(dh == 0), stop=(dh == 7))
                        ev1c = ev1l[f % 2]
                        P.ts(ev1c[:, :], pw[:, :], 0.0, None, OP.max)
                        P.tt(kkT[:, f, :], ev1c[:, :], ev1c[:, :], OP.mult)
                    for c in range(8):
                        pw = pwl[c % 2]
                        P.dma(wstr2[0][:, :, :], wcv.ap()[c, :, 0:16, :], eng="gpsimd")
                        P.dma(wstr2[1][:, :, :], wcv.ap()[c, :, 16:32, :], eng="gpsimd")
                        for f in range(32):
                            P.mm(pw[:, :], wstr2[f // 16][:, f % 16, :], kkT[:, f, :],
                                 start=(f == 0), stop=(f == 31))
                        P.copy(ev1[:, :], pw[:, :], eng="scalar")
                        for ti in range(8):
                            tp5 = tppsf5l[ti % 2]
                            P.tp(tp5[:, :], ev1[:, ti * 128:(ti + 1) * 128], identf[:, :])
                            P.copy(cvstage[:, ti, c * 128:(c + 1) * 128], tp5[:, :])
                    for ti in range(8):
                        tt_ = tg * 8 + ti
                        for dh in range(8):
                            P.mm(pw2[:, :], crTg[:, dh, ti * 128:(ti + 1) * 128],
                                 wcr_s[:, dh, :], start=(dh == 0), stop=(dh == 7))
                        P.act(ev2[:, :], pw2[:, :], AF.Sigmoid)
                        P.tt(ev2[:, :], ev2[:, :], cvstage[:, ti, :], OP.mult)
                        P.dma(xt[:, :], x1_d[tt_ * 128:(tt_ + 1) * 128, :], eng="scalar")
                        P.tt(ev2[:, :], ev2[:, :], xt[:, :], OP.add)
                        P.dma(s_dd[tt_ * 128:(tt_ + 1) * 128, :], ev2[:, :], eng="scalar")
                if "s" in tapset:
                    P.dma(tap_d["s"][:, :], s_dd[:, :])
            P.barrier()

        if stage >= 5:
            with ExitStack() as p6:
                hgT = sb(p6, "hgT", (128, 8, T), BF16)
                h1T = sb(p6, "h1T", (128, 32, 1024), BF16)
                w2stage = sb(p6, "w2stage", (128, 8, 1024))
                wstr6 = [sb(p6, "wstr6_a", (128, 8, 128), BF16),
                         sb(p6, "wstr6_b", (128, 8, 128), BF16)]
                wstr26 = [sb(p6, "wstr26_a", (128, 16, 128), BF16),
                          sb(p6, "wstr26_b", (128, 16, 128), BF16)]
                ev26 = sb(p6, "ev26", (128, 1024)); ev36 = sb(p6, "ev36", (128, 1024))
                gbf6 = sb(p6, "gbf6", (128, 1024), BF16)
                ev16 = sq_s
                xg6 = xt
                pgl = [psum(p6, "pg_a", (128, 1024)), psum(p6, "pg_b", (128, 1024))]
                tppsf6l = [psum(p6, "tppsf6a", (128, 128)), psum(p6, "tppsf6b", (128, 128))]
                tppsb6 = psum(p6, "tppsb6", (128, 128), BF16)

                for ti in range(16):
                    P.idma_gather(xg6[:, :], x_full[0:128, 0:C], idxcols[0:128, ti:ti + 1])
                    ln_stats(xg6[:, :], C)
                    P.ts(xg6[:, :], xg6[:, :], muc[:, 0:1], rsd[:, 0:1],
                         OP.subtract, OP.mult)
                    ln_stats(xg6[:, :], C)
                    P.ts(nxt[:, :], xg6[:, :], muc[:, 0:1], rsd[:, 0:1],
                         OP.subtract, OP.mult)
                    for cc in range(8):
                        P.tp(tppsb6[:, :], nxt[:, cc * 128:(cc + 1) * 128], identb[:, :])
                        P.copy(hgT[:, cc, ti * 128:(ti + 1) * 128], tppsb6[:, :])
                for cc in range(8):
                    P.ts(hgT[:, cc, :], hgT[:, cc, :], gfc[:, cc:cc + 1], bfc[:, cc:cc + 1],
                         OP.mult, OP.add)

                for tg in range(2):
                    t0 = tg * 1024
                    for f in range(32):
                        wb6 = wstr6[f % 2]
                        pg = pgl[f % 2]
                        P.dma(wb6[:, :, :], w1[f], eng="gpsimd")
                        for dh in range(8):
                            P.mm(pg[:, :], wb6[:, dh, :], hgT[:, dh, t0:t0 + 1024],
                                 start=(dh == 0), stop=(dh == 7))
                        ev16c = [ev16, xg6][f % 2]
                        P.ts(ev16c[:, :], pg[:, :], b1c[:, f:f + 1], None, OP.add)
                        P.tt(ev26[:, :], ev16c[:, :], ev16c[:, :], OP.mult)
                        P.ts(ev26[:, :], ev26[:, :], GC2, GC1, OP.mult, OP.add)
                        P.tt(ev36[:, :], ev16c[:, :], ev26[:, :], OP.mult)
                        P.act(ev36[:, :], ev36[:, :], AF.Sigmoid)
                        P.tt(h1T[:, f, :], ev16c[:, :], ev36[:, :], OP.mult)
                    for c in range(8):
                        pg = pgl[c % 2]
                        P.dma(wstr26[0][:, :, :], w2.ap()[c, :, 0:16, :], eng="gpsimd")
                        P.dma(wstr26[1][:, :, :], w2.ap()[c, :, 16:32, :], eng="gpsimd")
                        for f in range(32):
                            P.mm(pg[:, :], wstr26[f // 16][:, f % 16, :], h1T[:, f, :],
                                 start=(f == 0), stop=(f == 31))
                        P.ts(ev16[:, :], pg[:, :], b2fc[:, c:c + 1], None, OP.add)
                        for ti in range(8):
                            tp6 = tppsf6l[ti % 2]
                            P.tp(tp6[:, :], ev16[:, ti * 128:(ti + 1) * 128], identf[:, :])
                            P.copy(w2stage[:, ti, c * 128:(c + 1) * 128], tp6[:, :])
                    for ti in range(8):
                        tt_ = tg * 8 + ti
                        P.copy(gbf6[:, :], w2stage[:, ti, :])
                        P.dma(g_in[tt_ * 128:(tt_ + 1) * 128, :], gbf6[:, :])
                if "gin" in tapset:
                    P.dma(tap_d["gin"][:, :], g_in[:, :])

                P.cc("AllGather", OP.bypass, [g_in.ap().opt()], [g_out.ap().opt()])

                for ti in range(16):
                    P.idma_gather(gbf6[:, :], g_out[0:128, 0:C], myslots[0:128, ti:ti + 1])
                    P.dma(ev36[:, :], xn_d[ti * 128:(ti + 1) * 128, :])
                    P.tt(ev16[:, :], gbf6[:, :], ev36[:, :], OP.add)
                    P.act(ev26[:, :], ev16[:, :], AF.Sigmoid)
                    P.tt(ev26[:, :], ev16[:, :], ev26[:, :], OP.mult)
                    P.dma(xt[:, :], x_my[ti * 128:(ti + 1) * 128, :])
                    P.dma(ev36[:, :], s_dd[ti * 128:(ti + 1) * 128, :], eng="scalar")
                    P.tt(ev26[:, :], ev36[:, :], ev26[:, :], OP.mult)
                    P.tt(ev26[:, :], ev26[:, :], xt[:, :], OP.add)
                    P.dma(out_d[ti * 128:(ti + 1) * 128, :], ev26[:, :], eng="scalar")

        totals = dict(P.counts)
        for engname in ENGS:
            lst = P.ops[engname]
            if not lst:
                continue

            def make(lst):
                def _(h):
                    for fn, waits, semname, inc in lst:
                        for sm, v in waits.items():
                            h.wait_ge(sems[sm], v)
                        r = fn(h)
                        if inc:
                            r.then_inc(sems[semname], inc)
                return _
            getattr(blk, engname)(make(lst))

        @blk.sync
        def _(h):
            for sm, v in totals.items():
                if v > 0:
                    h.wait_ge(sems[sm], v)

    return nc


def _tile_sq(w):
    w = np.asarray(w, np.float32)
    return np.ascontiguousarray(w.reshape(8, 128, 1024).transpose(1, 0, 2))


def _tile_kf(w):
    w = np.asarray(w, np.float32)
    return np.ascontiguousarray(w.reshape(8, 128, 32, 128).transpose(2, 1, 0, 3))


def _tile_fk(w):
    w = np.asarray(w, np.float32)
    return np.ascontiguousarray(w.reshape(32, 128, 8, 128).transpose(2, 1, 0, 3))


def make_inputs(inputs, b):
    f32 = np.float32
    x = np.ascontiguousarray(np.asarray(inputs["L0_attn_x"], f32))
    col8 = lambda v: np.ascontiguousarray(np.asarray(v, f32).reshape(8, 128).T)
    m = {
        "x_my": np.ascontiguousarray(x[b]),
        "x_full": x.reshape(TT_ALL, C),
        "wtk": _tile_sq(inputs["L0_attn_w_tk"]),
        "wtv": _tile_sq(inputs["L0_attn_w_tv"]),
        "wtr": _tile_sq(inputs["L0_attn_w_tr"]),
        "wcr": _tile_sq(inputs["L0_attn_w_cr"]),
        "wck": _tile_kf(inputs["L0_attn_w_ck"]),
        "wcv": _tile_fk(inputs["L0_attn_w_cv"]),
        "w1": _tile_kf(np.asarray(inputs["L0_ffn_w1"], np.float32)[b]),
        "w2": _tile_fk(np.asarray(inputs["L0_ffn_w2"], np.float32)[b]),
        "b1c": np.ascontiguousarray(np.asarray(inputs["L0_ffn_b1"], f32)[b].reshape(32, 128).T),
        "b2fc": col8(np.asarray(inputs["L0_ffn_b2"], f32)[b]),
        "tdc": col8(inputs["L0_attn_time_decay"]),
        "tfc": col8(inputs["L0_attn_time_first"]),
        "mkc": col8(inputs["L0_attn_mix_k"]),
        "mvc": col8(inputs["L0_attn_mix_v"]),
        "mrc": col8(inputs["L0_attn_mix_r"]),
        "cmkc": col8(inputs["L0_attn_cmix_k"]),
        "cmrc": col8(inputs["L0_attn_cmix_r"]),
        "g1c": col8(inputs["L0_attn_ln1_g"]),
        "b1lc": col8(inputs["L0_attn_ln1_b"]),
        "g2c": col8(inputs["L0_attn_ln2_g"]),
        "b2lc": col8(inputs["L0_attn_ln2_b"]),
        "gfc": col8(inputs["L0_ffn_ln_g"]),
        "bfc": col8(inputs["L0_ffn_ln_b"]),
        "centT": np.ascontiguousarray(
            np.asarray(inputs["L0_ffn_centroids"], f32).T.reshape(8, 128, 8)
            .transpose(1, 0, 2)),
        "identf": np.eye(128, dtype=f32),
        "identb": np.eye(128, dtype=f32).astype(BF),
        "trie": np.triu(np.ones((128, 128), f32), 1),
        "onesf": np.ones((128, 128), f32),
        "sel16": None,
        "ohc": None,
    }
    sel = np.zeros((128, 16), f32)
    for i in range(16):
        sel[16 * b + i, i] = 1.0
    m["sel16"] = sel
    oh = np.zeros((128, 8), f32)
    oh[:, b] = 1.0
    m["ohc"] = oh
    return m


def kernel(**inputs):
    nc = build_nc(stage=9)
    in_maps = [make_inputs(inputs, b) for b in range(NCORE)]
    res = run_bass_kernel_spmd(nc, in_maps, core_ids=list(range(NCORE)))
    out = np.stack([np.asarray(res.results[b]["out"]) for b in range(NCORE)])
    return out.astype(np.float32)


# version 25
# speedup vs baseline: 1.0221x; 1.0130x over previous
import os
import numpy as np
import ml_dtypes
from contextlib import ExitStack

import concourse.bass as bass
import concourse.mybir as mybir
from concourse.bass_utils import run_bass_kernel_spmd

F32 = mybir.dt.float32
BF16 = mybir.dt.bfloat16
I32 = mybir.dt.int32
AF = mybir.ActivationFunctionType
OP = mybir.AluOpType
AXX = mybir.AxisListType.X
BF = ml_dtypes.bfloat16

NCORE = 8
T = 2048
C = 1024
FF = 4096
TT_ALL = NCORE * T
EPS = 1e-5
PPINIT = -40.0
NEGBIG = -3.0e38
POSBIG = 3.0e38
BIS_ITERS = 32
GC1 = 2.0 * 0.7978845608028654
GC2 = 2.0 * 0.7978845608028654 * 0.044715

ENGS = ("sync", "vector", "scalar", "gpsimd", "tensor")
SAFE_SAME_ENGINE = bool(int(os.environ.get("KSAFE", "1")))


def _regs(*aps):
    out = []
    for a in aps:
        if a is None or isinstance(a, (int, float)):
            continue
        t = getattr(a, "tensor", None)
        if t is not None:
            out.append(t.name)
    return out


class Prog:

    def __init__(self, nc):
        self.nc = nc
        self.sem_names = ["sync", "vector", "scalar", "gpsimd", "tensor", "gdma", "ccol"]
        self.counts = {s: 0 for s in self.sem_names}
        self.ops = {e: [] for e in ENGS}
        self.last_w = {}
        self.readers = {}
        self.guar = {e: {s: 0 for s in self.sem_names} for e in ENGS}
        self.floor = {}

    def barrier(self):
        self.floor = dict(self.counts)

    def emit_op(self, stream, fn, reads, writes, is_dma=False, sem=None, inc=None):
        sem = sem or stream
        inc = inc if inc is not None else (16 if is_dma else 1)
        needed = {}

        def add_dep(ev):
            if ev is None:
                return
            (s, sm, c, was_async, snap) = ev
            if s == stream and not was_async and (s == "tensor" or not SAFE_SAME_ENGINE):
                return
            if was_async:
                c = self.counts[sm]
            if self.guar[stream].get(sm, 0) >= c:
                return
            if needed.get(sm, 0) < c:
                needed[sm] = c

        for sm, c in self.floor.items():
            if c > 0 and self.guar[stream].get(sm, 0) < c and needed.get(sm, 0) < c:
                if not (sm == stream == sem and not is_dma):
                    needed[sm] = c
        for rg in reads:
            add_dep(self.last_w.get(rg))
        for rg in writes:
            add_dep(self.last_w.get(rg))
            for ev in self.readers.get(rg, ()):
                add_dep(ev)
        for sm, c in needed.items():
            ev = self._ev_by_sem.get((sm, c))
            if ev is not None:
                snap = ev[4]
                for s2, c2 in snap.items():
                    if self.guar[stream].get(s2, 0) < c2:
                        self.guar[stream][s2] = c2
            if self.guar[stream].get(sm, 0) < c:
                self.guar[stream][sm] = c
        self.counts[sem] += inc
        cnt = self.counts[sem]
        snap = dict(self.guar[stream])
        snap[sem] = cnt
        ev = (stream, sem, cnt, is_dma, snap)
        self._ev_by_sem[(sem, cnt)] = ev
        for rg in writes:
            self.last_w[rg] = ev
            self.readers[rg] = []
        for rg in reads:
            self.readers.setdefault(rg, []).append(ev)
        self.ops[stream].append((fn, dict(needed), sem, inc))

    _ev_by_sem = {}

    def dma(self, out, in_, eng="sync"):
        sem = "gdma" if eng == "gpsimd" else eng
        self.emit_op(eng, lambda e, out=out, in_=in_: e.dma_start(out=out, in_=in_),
                     _regs(in_), _regs(out), is_dma=True, sem=sem)

    def idma_gather(self, out, in_, idxcol):
        self.emit_op("gpsimd", lambda e, out=out, in_=in_, idx=idxcol:
                     e.indirect_dma_start(out, None, in_, bass.IndirectOffsetOnAxis(idx, 0)),
                     _regs(in_, idxcol), _regs(out), is_dma=True, sem="gdma")

    def tt(self, out, a, b, op, eng="vector"):
        self.emit_op(eng, lambda e, out=out, a=a, b=b, op=op: e.tensor_tensor(out, a, b, op),
                     _regs(a, b), _regs(out))

    def ts(self, out, a, s1, s2, op0, op1=None, accum=None, eng="vector"):
        if op1 is None:
            self.emit_op(eng, lambda e, out=out, a=a, s1=s1, op0=op0:
                         e.tensor_scalar(out, a, s1, None, op0),
                         _regs(a, s1), _regs(out))
        else:
            self.emit_op(eng, lambda e, out=out, a=a, s1=s1, s2=s2, op0=op0, op1=op1, ac=accum:
                         e.tensor_scalar(out, a, s1, s2, op0, op1, accum_out=ac),
                         _regs(a, s1, s2), _regs(out, accum))

    def stt(self, out, a, s, b, op0, op1, eng="vector"):
        self.emit_op(eng, lambda e, out=out, a=a, s=s, b=b, op0=op0, op1=op1:
                     e.scalar_tensor_tensor(out, a, s, b, op0, op1),
                     _regs(a, s, b), _regs(out))

    def act(self, out, in_, fn, bias=None, scale=None, accum=None):
        kw = {}
        if bias is not None:
            kw["bias"] = bias
        if scale is not None:
            kw["scale"] = scale
        if accum is not None:
            kw["accum_out"] = accum
        self.emit_op("scalar", lambda e, out=out, in_=in_, fn=fn, kw=kw:
                     e.activation(out, in_, fn, **kw),
                     _regs(in_, bias), _regs(out, accum))

    def copy(self, out, in_, eng="vector"):
        if eng == "scalar":
            self.emit_op(eng, lambda e, out=out, in_=in_:
                         e.activation(out, in_, AF.Copy),
                         _regs(in_), _regs(out))
        else:
            self.emit_op(eng, lambda e, out=out, in_=in_: e.tensor_copy(out, in_),
                         _regs(in_), _regs(out))

    def scan(self, out, d0, d1, init, eng="vector"):
        self.emit_op(eng, lambda e, out=out, d0=d0, d1=d1, init=init:
                     e.tensor_tensor_scan(out, d0, d1, init, OP.mult, OP.add),
                     _regs(d0, d1, init), _regs(out))

    def red(self, out, in_, op=OP.add, eng="vector"):
        self.emit_op(eng, lambda e, out=out, in_=in_, op=op:
                     e.tensor_reduce(out, in_, AXX, op),
                     _regs(in_), _regs(out))

    def mm(self, out, lhsT, rhs, start, stop):
        n = rhs.shape[-1]

        def fn(e, out=out, l=lhsT, r=rhs, st=start, sp=stop, n=n):
            if n <= 512:
                return e.matmul(out, l, r, start=st, stop=sp)
            ret = None
            for o in range(0, n, 512):
                w = min(512, n - o)
                ret = e.matmul(out[:, o:o + w], l, r[:, o:o + w], start=st, stop=sp)
            return ret
        reads = _regs(lhsT, rhs) + ([] if start else _regs(out))
        self.emit_op("tensor", fn, reads, _regs(out))

    def tp(self, out, in_, ident):
        self.emit_op("tensor", lambda e, out=out, in_=in_, id_=ident:
                     e.transpose(out, in_, id_),
                     _regs(in_, ident), _regs(out))

    def memset(self, ap, v, eng="gpsimd"):
        self.emit_op(eng, lambda e, ap=ap, v=v: e.memset(ap, v), [], _regs(ap))

    def iota(self, out, pattern, base, cm):
        self.emit_op("gpsimd", lambda e, out=out, p=pattern, b=base, cm=cm:
                     e.iota(out, pattern=p, base=b, channel_multiplier=cm,
                            allow_small_or_imprecise_dtypes=True),
                     [], _regs(out))

    def cc(self, kind, op, ins, outs):
        self.emit_op("gpsimd", lambda e, kind=kind, op=op, ins=ins, outs=outs:
                     e.collective_compute(kind, op, replica_groups=[list(range(NCORE))],
                                          ins=ins, outs=outs),
                     [t.tensor.name for t in ins], [t.tensor.name for t in outs],
                     is_dma=True, sem="ccol", inc=1)


def build_nc(stage=9, taps=()):
    tapset = set(taps)
    nc = bass.Bass("TRN2", target_bir_lowering=False)

    def din(name, shape, dt=F32):
        return nc.dram_tensor(name, list(shape), dt, kind="ExternalInput")

    x_my = din("x_my", (T, C))
    x_full = din("x_full", (TT_ALL, C))
    wtk = din("wtk", (128, 8, C)); wtv = din("wtv", (128, 8, C)); wtr = din("wtr", (128, 8, C))
    wcr = din("wcr", (128, 8, C))
    wck = din("wck", (32, 128, 8, 128)); wcv = din("wcv", (8, 128, 32, 128))
    w1 = din("w1", (32, 128, 8, 128)); w2 = din("w2", (8, 128, 32, 128))
    b1c_d = din("b1c", (128, 32))
    b2fc_d = din("b2fc", (128, 8))
    tdc_d = din("tdc", (128, 8)); tfc_d = din("tfc", (128, 8))
    mkc_d = din("mkc", (128, 8)); mvc_d = din("mvc", (128, 8)); mrc_d = din("mrc", (128, 8))
    cmkc_d = din("cmkc", (128, 8)); cmrc_d = din("cmrc", (128, 8))
    g1c_d = din("g1c", (128, 8)); b1lc_d = din("b1lc", (128, 8))
    g2c_d = din("g2c", (128, 8)); b2lc_d = din("b2lc", (128, 8))
    gfc_d = din("gfc", (128, 8)); bfc_d = din("bfc", (128, 8))
    centT_d = din("centT", (128, 8, 8))
    ident_d = din("identf", (128, 128))
    identb_d = din("identb", (128, 128), BF16)
    trie_d = din("trie", (128, 128))
    ones_d = din("onesf", (128, 128))
    sel16_d = din("sel16", (128, 16))
    ohc_d = din("ohc", (128, 8))

    out_d = nc.dram_tensor("out", [T, C], F32, kind="ExternalOutput")
    tap_d = {}

    def mktap(name, shape, dt=F32):
        if name in tapset:
            tap_d[name] = nc.dram_tensor("tap_" + name, list(shape), dt, kind="ExternalOutput")

    for nm, shp, dt in [("xn", (T, C), F32), ("scout", (64, T), F32),
                        ("masks", (128, 1024), F32), ("slot", (128, 128), F32),
                        ("idx", (128, 16), I32), ("myslot", (128, 16), I32),
                        ("r", (T, C), F32), ("x1", (T, C), F32), ("kdbg", (C, T), F32),
                        ("vdbg", (C, T), F32), ("ydbg", (C, T), F32), ("ppdbg", (128, 513), F32), ("aadbg", (128, 513), F32), ("dldbg", (128, 512), F32),
                        ("s", (T, C), F32), ("gin", (T, C), F32)]:
        mktap(nm, shp, dt)

    xn_d = nc.dram_tensor("xn_d", [T, C], F32)
    x1_d = nc.dram_tensor("x1_d", [T, C], F32)
    s_dd = nc.dram_tensor("s_dd", [T, C], F32)
    r_d = nc.dram_tensor("r_d", [T, C], F32)
    k_dbg_d = nc.dram_tensor("k_dbg_d", [C, T], F32) if "kdbg" in tapset else None
    v_dbg_d = nc.dram_tensor("v_dbg_d", [C, T], F32) if "vdbg" in tapset else None
    y_dbg_d = nc.dram_tensor("y_dbg_d", [C, T], F32) if "ydbg" in tapset else None
    xng_d = nc.dram_tensor("xng_d", [T, C], F32)
    rowcum_d = nc.dram_tensor("rowcum_d", [128, 128], F32)
    sc_in = nc.dram_tensor("sc_in", [8, T], F32)
    sc_out = nc.dram_tensor("sc_out", [8 * NCORE, T], F32)
    g_in = nc.dram_tensor("g_in", [T, C], BF16)
    g_out = nc.dram_tensor("g_out", [TT_ALL, C], BF16)

    st = ExitStack()
    with st:
        blk = st.enter_context(nc.Block())
        sems = {s: st.enter_context(nc.semaphore("sem_" + s))
                for s in ["sync", "vector", "scalar", "gpsimd", "tensor", "gdma", "ccol"]}
        P = Prog(nc)
        P._ev_by_sem = {}

        def sb(stk, name, shape, dt=F32):
            return stk.enter_context(nc.sbuf_tensor("sb_" + name, list(shape), dt))

        def psum(stk, name, shape, dt=F32):
            return stk.enter_context(nc.psum_tensor("ps_" + name, list(shape), dt))

        tdc = sb(st, "tdc", (128, 8)); tfc = sb(st, "tfc", (128, 8))
        ewc = sb(st, "ewc", (128, 8))
        mkc = sb(st, "mkc", (128, 8)); mvc = sb(st, "mvc", (128, 8)); mrc = sb(st, "mrc", (128, 8))
        cmkc = sb(st, "cmkc", (128, 8)); cmrc = sb(st, "cmrc", (128, 8))
        g1c = sb(st, "g1c", (128, 8)); b1lc = sb(st, "b1lc", (128, 8))
        g2c = sb(st, "g2c", (128, 8)); b2lc = sb(st, "b2lc", (128, 8))
        gfc = sb(st, "gfc", (128, 8)); bfc = sb(st, "bfc", (128, 8))
        b1c = sb(st, "b1c", (128, 32)); b2fc = sb(st, "b2fc", (128, 8))
        identf = sb(st, "identf", (128, 128)); identb = sb(st, "identb", (128, 128), BF16)
        trie = sb(st, "trie", (128, 128)); onesf = sb(st, "onesf", (128, 128))
        sel16 = sb(st, "sel16", (128, 16)); ohc = sb(st, "ohc", (128, 8))
        iot16 = sb(st, "iot16", (128, 16))
        idxcols = sb(st, "idxcols", (128, 16), I32)
        myslots = sb(st, "myslots", (128, 16), I32)
        s1c = sb(st, "s1c", (128, 1)); s2c = sb(st, "s2c", (128, 1))
        muc = sb(st, "muc", (128, 1)); varc = sb(st, "varc", (128, 1)); rsd = sb(st, "rsd", (128, 1))
        tva = sb(st, "tva", (128, 1)); tvb = sb(st, "tvb", (128, 1))
        xt = sb(st, "xt", (128, C)); sq_s = sb(st, "sq_s", (128, C))
        nxt = sb(st, "nxt", (128, C), BF16)
        c_eps = sb(st, "c_eps", (128, 1)); c_one = sb(st, "c_one", (128, 1))
        nx1T = sb(st, "nx1T", (128, 8, 2049), BF16)
        trunk = ExitStack()
        nxnT = sb(trunk, "nxnT", (128, 8, 2049), BF16)

        for tile, d in [(tdc, tdc_d), (tfc, tfc_d), (mkc, mkc_d), (mvc, mvc_d), (mrc, mrc_d),
                        (cmkc, cmkc_d), (cmrc, cmrc_d), (g1c, g1c_d), (b1lc, b1lc_d),
                        (g2c, g2c_d), (b2lc, b2lc_d), (gfc, gfc_d), (bfc, bfc_d),
                        (b1c, b1c_d), (b2fc, b2fc_d), (identf, ident_d), (identb, identb_d),
                        (trie, trie_d), (onesf, ones_d), (sel16, sel16_d), (ohc, ohc_d)]:
            P.dma(tile[:, :], d[:, :])
        P.iota(iot16[:, :], [[128, 16]], 0, 1)
        P.memset(c_eps[:, 0:1], float(EPS))
        P.memset(c_one[:, 0:1], 1.0)
        P.act(ewc[:, :], tdc[:, :], AF.Exp)

        def ln_stats(src, w):
            P.act(sq_s[:, 0:w], src, AF.Copy, accum=s1c[:, 0:1])
            P.tt(sq_s[:, 0:w], src, src, OP.mult)
            P.act(sq_s[:, 0:w], sq_s[:, 0:w], AF.Copy, accum=s2c[:, 0:1])
            P.ts(muc[:, 0:1], s1c[:, 0:1], 1.0 / w, None, OP.mult)
            P.tt(tva[:, 0:1], muc[:, 0:1], muc[:, 0:1], OP.mult)
            P.ts(varc[:, 0:1], s2c[:, 0:1], 1.0 / w, tva[:, 0:1], OP.mult, OP.subtract)
            P.act(tvb[:, 0:1], varc[:, 0:1], AF.Ln, bias=c_eps[:, 0:1])
            P.act(rsd[:, 0:1], tvb[:, 0:1], AF.Exp, scale=-0.5)

        with ExitStack() as p1:
            centTb = sb(p1, "centTb", (128, 8, 8), BF16)
            scev = sb(p1, "scev", (8, T))
            hfT_t = sb(p1, "hfT_t", (128, T), BF16)
            tppsb = psum(p1, "tppsb", (128, 128), BF16)
            scps = psum(p1, "scps", (8, T))
            P.dma(centTb[:, :, :], centT_d[:, :, :], eng="gpsimd")
            for cc in range(8):
                P.memset(nxnT[:, cc, 0:1], 0.0)
            for i in range(16):
                P.dma(xt[:, :], x_my[i * 128:(i + 1) * 128, :])
                ln_stats(xt[:, :], C)
                P.ts(xt[:, :], xt[:, :], muc[:, 0:1], rsd[:, 0:1], OP.subtract, OP.mult)
                P.dma(xn_d[i * 128:(i + 1) * 128, :], xt[:, :])
                ln_stats(xt[:, :], C)
                P.ts(nxt[:, :], xt[:, :], muc[:, 0:1], rsd[:, 0:1], OP.subtract, OP.mult)
                for cc in range(8):
                    P.tp(tppsb[:, :], nxt[:, cc * 128:(cc + 1) * 128], identb[:, :])
                    P.copy(nxnT[:, cc, 1 + i * 128:1 + (i + 1) * 128], tppsb[:, :])
            for cc in range(8):
                P.ts(hfT_t[:, :], nxnT[:, cc, 1:2049], gfc[:, cc:cc + 1], bfc[:, cc:cc + 1],
                     OP.mult, OP.add)
                for half in range(2):
                    P.mm(scps[0:8, half * 1024:(half + 1) * 1024], centTb[:, cc, :],
                         hfT_t[:, half * 1024:(half + 1) * 1024],
                         start=(cc == 0), stop=(cc == 7))
            P.copy(scev[0:8, :], scps[0:8, :])
            P.dma(sc_in[:, :], scev[0:8, :])
            P.cc("AllGather", OP.bypass, [sc_in.ap().opt()], [sc_out.ap().opt()])
            if "xn" in tapset:
                P.dma(tap_d["xn"][:, :], xn_d[:, :])
            if "scout" in tapset:
                P.dma(tap_d["scout"][:, :], sc_out[:, :])
        P.barrier()

        if stage >= 2:
            with ExitStack() as pr:
                s8 = sb(pr, "s8", (128, 1024))
                takenb = sb(pr, "takenb", (128, 128))
                masksb = sb(pr, "masksb", (128, 1024))
                slotb = sb(pr, "slotb", (128, 128))
                myrc = sb(pr, "myrc", (128, 128))
                myoc = sb(pr, "myoc", (128, 1))
                r_t1 = sb(pr, "r_t1", (128, 128)); r_t2 = sb(pr, "r_t2", (128, 128))
                r_t3 = sb(pr, "r_t3", (128, 128))
                rc1 = sb(pr, "rc1", (128, 1)); rc2 = sb(pr, "rc2", (128, 1))
                rc3 = sb(pr, "rc3", (128, 1)); rc4 = sb(pr, "rc4", (128, 1))
                rc5 = sb(pr, "rc5", (128, 1))
                lob = sb(pr, "lob", (128, 1)); wda = sb(pr, "wda", (128, 1))
                wdb = sb(pr, "wdb", (128, 1)); hib = sb(pr, "hib", (128, 1))
                lo0b = sb(pr, "lo0b", (128, 1)); hi0b = sb(pr, "hi0b", (128, 1))
                obb = sb(pr, "obb", (128, 128)); orow = sb(pr, "orow", (128, 128))
                pjc = sb(pr, "pjc", (128, 1)); ljc = sb(pr, "ljc", (128, 1))
                posc = sb(pr, "posc", (128, 1)); pji = sb(pr, "pji", (128, 1), I32)
                RG = sb(pr, "RG", (128, 128))
                cntps = psum(pr, "cntps", (128, 1))
                tppsf = psum(pr, "tppsf", (128, 128))
                selps = psum(pr, "selps", (16, 128))

                for e in range(8):
                    ap_in = bass.AP(sc_out, e * T, [[8 * T, 8], [128, 16], [1, 128]])
                    P.dma(s8[:, e * 128:(e + 1) * 128], ap_in)

                def part_reduce(vec_col, op, neutral, dst_col):
                    P.memset(r_t2[:, :], neutral)
                    P.copy(r_t2[:, 0:1], vec_col)
                    P.tp(tppsf[:, :], r_t2[:, :], identf[:, :])
                    P.copy(r_t3[:, :], tppsf[:, :])
                    P.red(rc2[0:1, 0:1], r_t3[0:1, 0:128], op)
                    P.mm(cntps[:, 0:1], onesf[0:1, :], rc2[0:1, 0:1], start=True, stop=True)
                    P.copy(dst_col, cntps[:, 0:1])

                P.red(rc1[:, 0:1], s8[:, :], OP.max)
                part_reduce(rc1[:, 0:1], OP.max, NEGBIG, hi0b[:, 0:1])
                P.ts(hi0b[:, 0:1], hi0b[:, 0:1], 1.0, None, OP.add)
                P.red(rc1[:, 0:1], s8[:, :], OP.min)
                part_reduce(rc1[:, 0:1], OP.min, POSBIG, lo0b[:, 0:1])
                P.ts(lo0b[:, 0:1], lo0b[:, 0:1], -1.0, None, OP.add)

                P.memset(takenb[:, :], 0.0)
                for e in range(8):
                    if e == 7:
                        P.ts(masksb[:, 7 * 128:8 * 128], takenb[:, :], -1.0, 1.0,
                             OP.mult, OP.add)
                        break
                    P.stt(r_t1[:, :], takenb[:, :], NEGBIG, s8[:, e * 128:(e + 1) * 128],
                          OP.mult, OP.add)
                    P.copy(lob[:, 0:1], lo0b[:, 0:1])
                    P.tt(wda[:, 0:1], hi0b[:, 0:1], lo0b[:, 0:1], OP.subtract)
                    wcur, wnext = wda, wdb
                    for it in range(BIS_ITERS):
                        P.ts(wnext[:, 0:1], wcur[:, 0:1], 0.5, None, OP.mult)
                        P.tt(rc3[:, 0:1], lob[:, 0:1], wnext[:, 0:1], OP.add)
                        P.ts(r_t2[:, :], r_t1[:, :], rc3[:, 0:1], None, OP.is_gt, OP.add,
                             accum=rc4[:, 0:1])
                        P.mm(cntps[:, 0:1], onesf[:, :], rc4[:, 0:1], start=True, stop=True)
                        P.ts(rc5[:, 0:1], cntps[:, 0:1], float(T), None, OP.is_ge)
                        P.tt(rc4[:, 0:1], rc5[:, 0:1], wnext[:, 0:1], OP.mult)
                        P.tt(lob[:, 0:1], lob[:, 0:1], rc4[:, 0:1], OP.add)
                        wcur, wnext = wnext, wcur
                    P.tt(hib[:, 0:1], lob[:, 0:1], wcur[:, 0:1], OP.add)
                    P.ts(r_t2[:, :], r_t1[:, :], hib[:, 0:1], None, OP.is_gt, OP.add,
                         accum=rc4[:, 0:1])
                    P.mm(cntps[:, 0:1], onesf[:, :], rc4[:, 0:1], start=True, stop=True)
                    P.ts(rc5[:, 0:1], cntps[:, 0:1], -1.0, float(T), OP.mult, OP.add)
                    P.ts(r_t3[:, :], r_t1[:, :], lob[:, 0:1], None, OP.is_gt)
                    P.tt(r_t3[:, :], r_t3[:, :], r_t2[:, :], OP.subtract)
                    P.scan(r_t1[:, :], onesf[:, :], r_t3[:, :], 0.0)
                    P.mm(cntps[:, 0:1], trie[:, :], r_t1[:, 127:128], start=True, stop=True)
                    P.copy(rc1[:, 0:1], cntps[:, 0:1])
                    P.ts(r_t1[:, :], r_t1[:, :], rc1[:, 0:1], None, OP.add)
                    P.ts(r_t1[:, :], r_t1[:, :], rc5[:, 0:1], None, OP.is_le)
                    P.tt(r_t3[:, :], r_t3[:, :], r_t1[:, :], OP.mult)
                    P.tt(masksb[:, e * 128:(e + 1) * 128], r_t2[:, :], r_t3[:, :], OP.add)
                    P.tt(takenb[:, :], takenb[:, :], masksb[:, e * 128:(e + 1) * 128], OP.add)

                P.memset(slotb[:, :], 0.0)
                P.memset(myrc[:, :], 0.0)
                P.memset(myoc[:, 0:1], 0.0)
                for e in range(8):
                    me = masksb[:, e * 128:(e + 1) * 128]
                    P.scan(r_t2[:, :], onesf[:, :], me, 0.0)
                    P.mm(cntps[:, 0:1], trie[:, :], r_t2[:, 127:128], start=True, stop=True)
                    P.copy(rc1[:, 0:1], cntps[:, 0:1])
                    P.ts(r_t3[:, :], r_t2[:, :], rc1[:, 0:1], None, OP.add)
                    P.ts(r_t3[:, :], r_t3[:, :], float(e * T - 1), None, OP.add)
                    P.tt(r_t3[:, :], r_t3[:, :], me, OP.mult)
                    P.tt(slotb[:, :], slotb[:, :], r_t3[:, :], OP.add)
                    P.ts(r_t3[:, :], r_t2[:, :], ohc[:, e:e + 1], None, OP.mult)
                    P.tt(myrc[:, :], myrc[:, :], r_t3[:, :], OP.add)
                    P.ts(rc2[:, 0:1], rc1[:, 0:1], ohc[:, e:e + 1], None, OP.mult)
                    P.tt(myoc[:, 0:1], myoc[:, 0:1], rc2[:, 0:1], OP.add)

                P.mm(selps[0:16, :], sel16[:, :], slotb[:, :], start=True, stop=True)
                P.copy(r_t2[0:16, :], selps[0:16, :])
                P.tp(tppsf[:, :], r_t2[:, :], identf[:, :])
                P.copy(r_t3[:, :], tppsf[:, :])
                P.copy(myslots[:, :], r_t3[:, 0:16])

                P.dma(rowcum_d[:, :], myrc[:, :])
                P.memset(r_t2[:, :], 0.0)
                P.copy(r_t2[:, 0:1], myoc[:, 0:1])
                P.tp(tppsf[:, :], r_t2[:, :], identf[:, :])
                P.copy(orow[:, :], tppsf[:, :])
                P.mm(tppsf[:, :], onesf[0:1, :], orow[0:1, :], start=True, stop=True)
                P.copy(obb[:, :], tppsf[:, :])
                for i in range(16):
                    jcol = iot16[:, i:i + 1]
                    P.ts(r_t2[:, :], obb[:, :], jcol, None, OP.is_le, OP.add,
                         accum=pjc[:, 0:1])
                    P.ts(pjc[:, 0:1], pjc[:, 0:1], -1.0, None, OP.add)
                    P.tt(r_t3[:, :], obb[:, :], r_t2[:, :], OP.mult)
                    P.red(rc1[:, 0:1], r_t3[:, :], OP.max)
                    P.tt(ljc[:, 0:1], jcol, rc1[:, 0:1], OP.subtract)
                    P.copy(pji[:, 0:1], pjc[:, 0:1])
                    P.idma_gather(RG[:, :], rowcum_d[0:128, 0:128], pji[0:128, 0:1])
                    P.ts(r_t2[:, :], RG[:, :], ljc[:, 0:1], None, OP.is_le, OP.add,
                         accum=posc[:, 0:1])
                    P.ts(rc1[:, 0:1], pjc[:, 0:1], 128.0, posc[:, 0:1], OP.mult, OP.add)
                    P.copy(idxcols[:, i:i + 1], rc1[:, 0:1])

                if "masks" in tapset:
                    P.dma(tap_d["masks"][:, :], masksb[:, :])
                if "slot" in tapset:
                    P.dma(tap_d["slot"][:, :], slotb[:, :])
                if "idx" in tapset:
                    P.dma(tap_d["idx"][:, :], idxcols[:, :])
                if "myslot" in tapset:
                    P.dma(tap_d["myslot"][:, :], myslots[:, :])
            P.barrier()

        if stage >= 3:
            with ExitStack() as p4:
                wtk_s = sb(p4, "wtk_s", (128, 8, C), BF16)
                wtv_s = sb(p4, "wtv_s", (128, 8, C), BF16)
                mixd = sb(p4, "mixd", (128, 512), BF16)
                xkTg = sb(p4, "xkTg", (128, 8, 512), BF16)
                xvTg = sb(p4, "xvTg", (128, 8, 512), BF16)
                kbuf = sb(p4, "kbuf", (128, 512))
                vbuf = sb(p4, "vbuf", (128, 512))
                yTtg = sb(p4, "yTtg", (128, 8, 512))
                ones512 = sb(p4, "ones512", (128, 512))
                ppt0 = sb(p4, "ppt0", (128, 513))
                aat0 = sb(p4, "aat0", (128, 513))
                ppcar = sb(p4, "ppcar", (128, 8)); aacar = sb(p4, "aacar", (128, 8))
                wpc = sb(p4, "wpc", (128, 1))
                sc1 = sb(p4, "sc1", (128, 512)); sc2 = sb(p4, "sc2", (128, 512))
                sc3 = sb(p4, "sc3", (128, 512)); sc4 = sb(p4, "sc4", (128, 512))
                sc5 = sb(p4, "sc5", (128, 512)); sc6 = sb(p4, "sc6", (128, 512))
                yt_f = sb(p4, "yt_f", (128, C)); rt_f = sb(p4, "rt_f", (128, C))
                x1t = sb(p4, "x1t", (128, C))
                mmps_a = psum(p4, "mmps_a", (128, 512))
                mmps_b = psum(p4, "mmps_b", (128, 512))
                mmps_w = psum(p4, "mmps_w", (128, 1024))
                tppsf4 = psum(p4, "tppsf4", (128, 128))
                tppsb2 = psum(p4, "tppsb2", (128, 128), BF16)

                for wsb, wd in [(wtk_s, wtk), (wtv_s, wtv)]:
                    P.dma(wsb[:, :, :], wd[:, :, :], eng="gpsimd")
                P.memset(ones512[:, :], 1.0)
                P.memset(ppcar[:, :], 0.0)
                P.memset(aacar[:, :], 0.0)

                for cc in range(8):
                    P.ts(nxnT[:, cc, 1:2049], nxnT[:, cc, 1:2049], g1c[:, cc:cc + 1],
                         b1lc[:, cc:cc + 1], OP.mult, OP.add)
                hT = nxnT

                with ExitStack() as p3:
                    wtr_s = sb(p3, "wtr_s", (128, 8, C), BF16)
                    xr_t = sb(p3, "xr_t", (128, 8, 128), BF16)
                    P.dma(wtr_s[:, :, :], wtr[:, :, :], eng="gpsimd")
                    for ti in range(16):
                        for cc in range(8):
                            hp = hT[:, cc, ti * 128:(ti + 1) * 128]
                            hh = hT[:, cc, 1 + ti * 128:1 + (ti + 1) * 128]
                            P.tt(xr_t[:, cc, :], hh, hp, OP.subtract)
                            P.stt(xr_t[:, cc, :], xr_t[:, cc, :], mrc[:, cc:cc + 1], hp,
                                  OP.mult, OP.add)
                        for dh in range(8):
                            P.mm(mmps_w[:, :], xr_t[:, dh, :], wtr_s[:, dh, :],
                                 start=(dh == 0), stop=(dh == 7))
                        P.act(rt_f[:, :], mmps_w[:, :], AF.Sigmoid)
                        P.dma(r_d[ti * 128:(ti + 1) * 128, :], rt_f[:, :])
                if "r" in tapset:
                    P.dma(tap_d["r"][:, :], r_d[:, :])

                for cc in range(8):
                    P.memset(nx1T[:, cc, 0:1], 0.0)
                for tg in range(4):
                    t0 = tg * 512
                    for cc in range(8):
                        hp = hT[:, cc, t0:t0 + 512]
                        hh = hT[:, cc, 1 + t0:1 + t0 + 512]
                        P.tt(mixd[:, :], hh, hp, OP.subtract)
                        P.stt(xkTg[:, cc, :], mixd[:, :], mkc[:, cc:cc + 1], hp,
                              OP.mult, OP.add)
                        P.stt(xvTg[:, cc, :], mixd[:, :], mvc[:, cc:cc + 1], hp,
                              OP.mult, OP.add)
                    for cc in range(8):
                        for dh in range(8):
                            P.mm(mmps_a[:, :], wtk_s[:, dh, cc * 128:(cc + 1) * 128],
                                 xkTg[:, dh, :], start=(dh == 0), stop=(dh == 7))
                        P.copy(kbuf[:, :], mmps_a[:, :], eng="scalar")
                        for dh in range(8):
                            P.mm(mmps_b[:, :], wtv_s[:, dh, cc * 128:(cc + 1) * 128],
                                 xvTg[:, dh, :], start=(dh == 0), stop=(dh == 7))
                        P.copy(vbuf[:, :], mmps_b[:, :], eng="scalar")
                        if k_dbg_d is not None:
                            P.dma(k_dbg_d[cc * 128:(cc + 1) * 128, t0:t0 + 512], kbuf[:, :])
                        if v_dbg_d is not None:
                            P.dma(v_dbg_d[cc * 128:(cc + 1) * 128, t0:t0 + 512], vbuf[:, :])
                        kk_ = kbuf[:, :]; vv_ = vbuf[:, :]
                        pc = ppcar[:, cc:cc + 1]; ac = aacar[:, cc:cc + 1]
                        pt = ppt0; at = aat0
                        P.ts(sc6[:, :], ones512[:, :], ewc[:, cc:cc + 1], None, OP.mult)
                        P.ts(wpc[:, 0:1], tdc[:, cc:cc + 1], pc, None, OP.subtract)
                        P.ts(sc1[:, :], kk_, wpc[:, 0:1], None, OP.add)
                        P.act(sc2[:, :], sc1[:, :], AF.Exp)
                        P.scan(sc3[:, :], sc6[:, :], sc2[:, :], 0.0 if tg == 0 else 1.0)
                        if cc == 0 and tg == 0 and "dldbg" in tapset:
                            P.dma(tap_d["dldbg"][:, :], sc3[:, :])
                        P.act(sc4[:, :], sc3[:, :], AF.Ln)
                        if tg == 0:
                            P.memset(pt[:, 0:1], PPINIT)
                        else:
                            P.copy(pt[:, 0:1], pc)
                        P.ts(pt[:, 1:513], sc4[:, :], pc, None, OP.add)
                        P.copy(ppcar[:, cc:cc + 1], pt[:, 512:513])
                        P.tt(sc1[:, :], pt[:, 0:512], kk_, OP.subtract)
                        P.ts(sc2[:, :], sc1[:, :], 0.0, None, OP.min)
                        P.act(sc3[:, :], sc2[:, :], AF.Exp)
                        P.ts(sc2[:, :], sc1[:, :], 0.0, None, OP.max)
                        P.act(sc4[:, :], sc2[:, :], AF.Exp, scale=-1.0)
                        P.tt(sc5[:, :], sc4[:, :], vv_, OP.mult, eng="gpsimd")
                        P.copy(at[:, 0:1], ac)
                        P.scan(at[:, 1:513], sc3[:, :], sc5[:, :], ac)
                        P.copy(aacar[:, cc:cc + 1], at[:, 512:513])
                        P.ts(sc1[:, :], sc1[:, :], tfc[:, cc:cc + 1], None, OP.subtract)
                        P.ts(sc2[:, :], sc1[:, :], 0.0, None, OP.min)
                        P.act(sc3[:, :], sc2[:, :], AF.Exp)
                        P.ts(sc2[:, :], sc1[:, :], 0.0, None, OP.max)
                        P.act(sc4[:, :], sc2[:, :], AF.Exp, scale=-1.0)
                        P.tt(sc5[:, :], sc3[:, :], at[:, 0:512], OP.mult)
                        P.tt(sc2[:, :], sc4[:, :], vv_, OP.mult, eng="gpsimd")
                        P.tt(sc5[:, :], sc5[:, :], sc2[:, :], OP.add)
                        P.tt(sc2[:, :], sc3[:, :], sc4[:, :], OP.mult, eng="gpsimd")
                        P.act(sc3[:, :], sc2[:, :], AF.Ln, bias=c_one[:, 0:1])
                        P.act(sc2[:, :], sc3[:, :], AF.Exp, scale=-1.0)
                        P.tt(yTtg[:, cc, :], sc5[:, :], sc2[:, :], OP.mult)
                        if y_dbg_d is not None:
                            P.dma(y_dbg_d[cc * 128:(cc + 1) * 128, t0:t0 + 512],
                                  yTtg[:, cc, :])
                        if cc == 0 and tg == 0 and "ppdbg" in tapset:
                            P.dma(tap_d["ppdbg"][:, :], pt[:, :])
                            P.dma(tap_d["aadbg"][:, :], at[:, :])
                    for ti in range(4):
                        tt_ = tg * 4 + ti
                        for cc in range(8):
                            P.tp(tppsf4[:, :], yTtg[:, cc, ti * 128:(ti + 1) * 128],
                                 identf[:, :])
                            P.copy(yt_f[:, cc * 128:(cc + 1) * 128], tppsf4[:, :])
                        P.dma(rt_f[:, :], r_d[tt_ * 128:(tt_ + 1) * 128, :])
                        P.dma(xt[:, :], xn_d[tt_ * 128:(tt_ + 1) * 128, :])
                        P.tt(x1t[:, :], rt_f[:, :], yt_f[:, :], OP.mult)
                        P.tt(x1t[:, :], x1t[:, :], xt[:, :], OP.add)
                        P.dma(x1_d[tt_ * 128:(tt_ + 1) * 128, :], x1t[:, :])
                        ln_stats(x1t[:, :], C)
                        P.ts(nxt[:, :], x1t[:, :], muc[:, 0:1], rsd[:, 0:1],
                             OP.subtract, OP.mult)
                        for cc in range(8):
                            P.tp(tppsb2[:, :], nxt[:, cc * 128:(cc + 1) * 128], identb[:, :])
                            P.copy(nx1T[:, cc, 1 + tt_ * 128:1 + (tt_ + 1) * 128],
                                   tppsb2[:, :])
                if "x1" in tapset:
                    P.dma(tap_d["x1"][:, :], x1_d[:, :])
                if k_dbg_d is not None:
                    P.dma(tap_d["kdbg"][:, :], k_dbg_d[:, :])
                if v_dbg_d is not None:
                    P.dma(tap_d["vdbg"][:, :], v_dbg_d[:, :])
                if y_dbg_d is not None:
                    P.dma(tap_d["ydbg"][:, :], y_dbg_d[:, :])

            P.barrier()
        trunk.close()

        if stage >= 4:
            with ExitStack() as p5:
                wcr_s = sb(p5, "wcr_s", (128, 8, C), BF16)
                ckTg = sb(p5, "ckTg", (128, 8, 1024), BF16)
                crTg = sb(p5, "crTg", (128, 8, 1024), BF16)
                kkT = sb(p5, "kkT", (128, 32, 1024), BF16)
                cvstage = sb(p5, "cvstage", (128, 8, 1024))
                wstr = [sb(p5, "wstr_a", (128, 8, 128), BF16),
                        sb(p5, "wstr_b", (128, 8, 128), BF16),
                        sb(p5, "wstr_c", (128, 8, 128), BF16)]
                wstr2 = [sb(p5, "wstr2_a", (128, 16, 128), BF16),
                         sb(p5, "wstr2_b", (128, 16, 128), BF16)]
                dscr = nxt
                ev2 = sb(p5, "ev2", (128, 1024))
                ev1l = [sq_s, ev2]
                ev1 = sq_s
                pwl = [psum(p5, "pw_a", (128, 1024)), psum(p5, "pw_b", (128, 1024))]
                pw2 = psum(p5, "pw2", (128, 1024))
                tppsf5l = [psum(p5, "tppsf5a", (128, 128)), psum(p5, "tppsf5b", (128, 128))]

                P.dma(wcr_s[:, :, :], wcr[:, :, :], eng="gpsimd")
                for cc in range(8):
                    P.ts(nx1T[:, cc, 1:2049], nx1T[:, cc, 1:2049], g2c[:, cc:cc + 1],
                         b2lc[:, cc:cc + 1], OP.mult, OP.add)
                for tg in range(2):
                    t0 = tg * 1024
                    for cc in range(8):
                        hp = nx1T[:, cc, t0:t0 + 1024]
                        hh = nx1T[:, cc, 1 + t0:1 + t0 + 1024]
                        P.tt(dscr[:, :], hh, hp, OP.subtract)
                        P.stt(ckTg[:, cc, :], dscr[:, :], cmkc[:, cc:cc + 1], hp,
                              OP.mult, OP.add)
                        P.stt(crTg[:, cc, :], dscr[:, :], cmrc[:, cc:cc + 1], hp,
                              OP.mult, OP.add)
                    for f in range(32):
                        wb = wstr[f % 3]
                        pw = pwl[f % 2]
                        P.dma(wb[:, :, :], wck[f], eng="gpsimd")
                        for dh in range(8):
                            P.mm(pw[:, :], wb[:, dh, :], ckTg[:, dh, :],
                                 start=(dh == 0), stop=(dh == 7))
                        ev1c = ev1l[f % 2]
                        P.ts(ev1c[:, :], pw[:, :], 0.0, None, OP.max)
                        P.tt(kkT[:, f, :], ev1c[:, :], ev1c[:, :], OP.mult)
                    for c in range(8):
                        pw = pwl[c % 2]
                        P.dma(wstr2[0][:, :, :], wcv.ap()[c, :, 0:16, :], eng="gpsimd")
                        P.dma(wstr2[1][:, :, :], wcv.ap()[c, :, 16:32, :], eng="gpsimd")
                        for f in range(32):
                            P.mm(pw[:, :], wstr2[f // 16][:, f % 16, :], kkT[:, f, :],
                                 start=(f == 0), stop=(f == 31))
                        P.copy(ev1[:, :], pw[:, :], eng="scalar")
                        for ti in range(8):
                            tp5 = tppsf5l[ti % 2]
                            P.tp(tp5[:, :], ev1[:, ti * 128:(ti + 1) * 128], identf[:, :])
                            P.copy(cvstage[:, ti, c * 128:(c + 1) * 128], tp5[:, :])
                    for ti in range(8):
                        tt_ = tg * 8 + ti
                        for dh in range(8):
                            P.mm(pw2[:, :], crTg[:, dh, ti * 128:(ti + 1) * 128],
                                 wcr_s[:, dh, :], start=(dh == 0), stop=(dh == 7))
                        P.act(ev2[:, :], pw2[:, :], AF.Sigmoid)
                        P.tt(ev2[:, :], ev2[:, :], cvstage[:, ti, :], OP.mult)
                        P.dma(xt[:, :], x1_d[tt_ * 128:(tt_ + 1) * 128, :])
                        P.tt(ev2[:, :], ev2[:, :], xt[:, :], OP.add)
                        P.dma(s_dd[tt_ * 128:(tt_ + 1) * 128, :], ev2[:, :])
                if "s" in tapset:
                    P.dma(tap_d["s"][:, :], s_dd[:, :])
            P.barrier()

        if stage >= 5:
            with ExitStack() as p6:
                hgT = sb(p6, "hgT", (128, 8, T), BF16)
                h1T = sb(p6, "h1T", (128, 32, 1024), BF16)
                w2stage = sb(p6, "w2stage", (128, 8, 1024))
                wstr6 = [sb(p6, "wstr6_a", (128, 8, 128), BF16),
                         sb(p6, "wstr6_b", (128, 8, 128), BF16),
                         sb(p6, "wstr6_c", (128, 8, 128), BF16)]
                wstr26 = [sb(p6, "wstr26_a", (128, 16, 128), BF16),
                          sb(p6, "wstr26_b", (128, 16, 128), BF16)]
                ev26 = sb(p6, "ev26", (128, 1024)); ev36 = sb(p6, "ev36", (128, 1024))
                gbf6 = sb(p6, "gbf6", (128, 1024), BF16)
                ev16 = sq_s
                xg6 = xt
                pgl = [psum(p6, "pg_a", (128, 1024)), psum(p6, "pg_b", (128, 1024))]
                tppsf6l = [psum(p6, "tppsf6a", (128, 128)), psum(p6, "tppsf6b", (128, 128))]
                tppsb6 = psum(p6, "tppsb6", (128, 128), BF16)

                for ti in range(16):
                    P.idma_gather(xg6[:, :], x_full[0:128, 0:C], idxcols[0:128, ti:ti + 1])
                    ln_stats(xg6[:, :], C)
                    P.ts(xg6[:, :], xg6[:, :], muc[:, 0:1], rsd[:, 0:1],
                         OP.subtract, OP.mult)
                    ln_stats(xg6[:, :], C)
                    P.ts(nxt[:, :], xg6[:, :], muc[:, 0:1], rsd[:, 0:1],
                         OP.subtract, OP.mult)
                    for cc in range(8):
                        P.tp(tppsb6[:, :], nxt[:, cc * 128:(cc + 1) * 128], identb[:, :])
                        P.copy(hgT[:, cc, ti * 128:(ti + 1) * 128], tppsb6[:, :])
                for cc in range(8):
                    P.ts(hgT[:, cc, :], hgT[:, cc, :], gfc[:, cc:cc + 1], bfc[:, cc:cc + 1],
                         OP.mult, OP.add)

                for tg in range(2):
                    t0 = tg * 1024
                    for f in range(32):
                        wb6 = wstr6[f % 3]
                        pg = pgl[f % 2]
                        P.dma(wb6[:, :, :], w1[f], eng="gpsimd")
                        for dh in range(8):
                            P.mm(pg[:, :], wb6[:, dh, :], hgT[:, dh, t0:t0 + 1024],
                                 start=(dh == 0), stop=(dh == 7))
                        ev16c = [ev16, xg6][f % 2]
                        P.ts(ev16c[:, :], pg[:, :], b1c[:, f:f + 1], None, OP.add)
                        P.tt(ev26[:, :], ev16c[:, :], ev16c[:, :], OP.mult)
                        P.ts(ev26[:, :], ev26[:, :], GC2, GC1, OP.mult, OP.add)
                        P.tt(ev36[:, :], ev16c[:, :], ev26[:, :], OP.mult)
                        P.act(ev36[:, :], ev36[:, :], AF.Sigmoid)
                        P.tt(h1T[:, f, :], ev16c[:, :], ev36[:, :], OP.mult)
                    for c in range(8):
                        pg = pgl[c % 2]
                        P.dma(wstr26[0][:, :, :], w2.ap()[c, :, 0:16, :], eng="gpsimd")
                        P.dma(wstr26[1][:, :, :], w2.ap()[c, :, 16:32, :], eng="gpsimd")
                        for f in range(32):
                            P.mm(pg[:, :], wstr26[f // 16][:, f % 16, :], h1T[:, f, :],
                                 start=(f == 0), stop=(f == 31))
                        P.ts(ev16[:, :], pg[:, :], b2fc[:, c:c + 1], None, OP.add)
                        for ti in range(8):
                            tp6 = tppsf6l[ti % 2]
                            P.tp(tp6[:, :], ev16[:, ti * 128:(ti + 1) * 128], identf[:, :])
                            P.copy(w2stage[:, ti, c * 128:(c + 1) * 128], tp6[:, :])
                    for ti in range(8):
                        tt_ = tg * 8 + ti
                        P.copy(gbf6[:, :], w2stage[:, ti, :])
                        P.dma(g_in[tt_ * 128:(tt_ + 1) * 128, :], gbf6[:, :])
                if "gin" in tapset:
                    P.dma(tap_d["gin"][:, :], g_in[:, :])

                P.cc("AllGather", OP.bypass, [g_in.ap().opt()], [g_out.ap().opt()])

                for ti in range(16):
                    P.idma_gather(gbf6[:, :], g_out[0:128, 0:C], myslots[0:128, ti:ti + 1])
                    P.dma(ev36[:, :], xn_d[ti * 128:(ti + 1) * 128, :])
                    P.tt(ev16[:, :], gbf6[:, :], ev36[:, :], OP.add)
                    P.act(ev26[:, :], ev16[:, :], AF.Sigmoid)
                    P.tt(ev26[:, :], ev16[:, :], ev26[:, :], OP.mult)
                    P.dma(xt[:, :], x_my[ti * 128:(ti + 1) * 128, :])
                    P.dma(ev36[:, :], s_dd[ti * 128:(ti + 1) * 128, :])
                    P.tt(ev26[:, :], ev36[:, :], ev26[:, :], OP.mult)
                    P.tt(ev26[:, :], ev26[:, :], xt[:, :], OP.add)
                    P.dma(out_d[ti * 128:(ti + 1) * 128, :], ev26[:, :])

        totals = dict(P.counts)
        for engname in ENGS:
            lst = P.ops[engname]
            if not lst:
                continue

            def make(lst):
                def _(h):
                    for fn, waits, semname, inc in lst:
                        for sm, v in waits.items():
                            h.wait_ge(sems[sm], v)
                        r = fn(h)
                        if inc:
                            r.then_inc(sems[semname], inc)
                return _
            getattr(blk, engname)(make(lst))

        @blk.sync
        def _(h):
            for sm, v in totals.items():
                if v > 0:
                    h.wait_ge(sems[sm], v)

    return nc


def _tile_sq(w):
    w = np.asarray(w, np.float32)
    return np.ascontiguousarray(w.reshape(8, 128, 1024).transpose(1, 0, 2))


def _tile_kf(w):
    w = np.asarray(w, np.float32)
    return np.ascontiguousarray(w.reshape(8, 128, 32, 128).transpose(2, 1, 0, 3))


def _tile_fk(w):
    w = np.asarray(w, np.float32)
    return np.ascontiguousarray(w.reshape(32, 128, 8, 128).transpose(2, 1, 0, 3))


def make_inputs(inputs, b):
    f32 = np.float32
    x = np.ascontiguousarray(np.asarray(inputs["L0_attn_x"], f32))
    col8 = lambda v: np.ascontiguousarray(np.asarray(v, f32).reshape(8, 128).T)
    m = {
        "x_my": np.ascontiguousarray(x[b]),
        "x_full": x.reshape(TT_ALL, C),
        "wtk": _tile_sq(inputs["L0_attn_w_tk"]),
        "wtv": _tile_sq(inputs["L0_attn_w_tv"]),
        "wtr": _tile_sq(inputs["L0_attn_w_tr"]),
        "wcr": _tile_sq(inputs["L0_attn_w_cr"]),
        "wck": _tile_kf(inputs["L0_attn_w_ck"]),
        "wcv": _tile_fk(inputs["L0_attn_w_cv"]),
        "w1": _tile_kf(np.asarray(inputs["L0_ffn_w1"], np.float32)[b]),
        "w2": _tile_fk(np.asarray(inputs["L0_ffn_w2"], np.float32)[b]),
        "b1c": np.ascontiguousarray(np.asarray(inputs["L0_ffn_b1"], f32)[b].reshape(32, 128).T),
        "b2fc": col8(np.asarray(inputs["L0_ffn_b2"], f32)[b]),
        "tdc": col8(inputs["L0_attn_time_decay"]),
        "tfc": col8(inputs["L0_attn_time_first"]),
        "mkc": col8(inputs["L0_attn_mix_k"]),
        "mvc": col8(inputs["L0_attn_mix_v"]),
        "mrc": col8(inputs["L0_attn_mix_r"]),
        "cmkc": col8(inputs["L0_attn_cmix_k"]),
        "cmrc": col8(inputs["L0_attn_cmix_r"]),
        "g1c": col8(inputs["L0_attn_ln1_g"]),
        "b1lc": col8(inputs["L0_attn_ln1_b"]),
        "g2c": col8(inputs["L0_attn_ln2_g"]),
        "b2lc": col8(inputs["L0_attn_ln2_b"]),
        "gfc": col8(inputs["L0_ffn_ln_g"]),
        "bfc": col8(inputs["L0_ffn_ln_b"]),
        "centT": np.ascontiguousarray(
            np.asarray(inputs["L0_ffn_centroids"], f32).T.reshape(8, 128, 8)
            .transpose(1, 0, 2)),
        "identf": np.eye(128, dtype=f32),
        "identb": np.eye(128, dtype=f32).astype(BF),
        "trie": np.triu(np.ones((128, 128), f32), 1),
        "onesf": np.ones((128, 128), f32),
        "sel16": None,
        "ohc": None,
    }
    sel = np.zeros((128, 16), f32)
    for i in range(16):
        sel[16 * b + i, i] = 1.0
    m["sel16"] = sel
    oh = np.zeros((128, 8), f32)
    oh[:, b] = 1.0
    m["ohc"] = oh
    return m


def kernel(**inputs):
    nc = build_nc(stage=9)
    in_maps = [make_inputs(inputs, b) for b in range(NCORE)]
    res = run_bass_kernel_spmd(nc, in_maps, core_ids=list(range(NCORE)))
    out = np.stack([np.asarray(res.results[b]["out"]) for b in range(NCORE)])
    return out.astype(np.float32)
